# version 15
# baseline (speedup 1.0000x reference)
import math
import numpy as np
import ml_dtypes
import concourse.bass as bass
import concourse.mybir as mybir
from concourse.bass_utils import run_bass_kernel_spmd

F32 = mybir.dt.float32
BF16 = mybir.dt.bfloat16
U8 = mybir.dt.uint8
I32 = mybir.dt.int32
AF = mybir.ActivationFunctionType
ALU = mybir.AluOpType
BFNP = ml_dtypes.bfloat16

P = 128
S = 8192
D = 4096
NH = 32
DH = 128
SHARD = 1024
NCORE = 8
NIT = 17
NSLOT = 24
SCALE = 1.0 / math.sqrt(DH)
ROFF = (0, 129, 258, 512)
NEG = -1.0e30


def _alibi_slopes(n=NH):
    cp2 = 2 ** int(math.floor(math.log2(n)))
    base = 2.0 ** (-(2.0 ** (-(math.log2(cp2) - 3))))
    slopes = base ** np.arange(1, cp2 + 1, dtype=np.float64)
    if cp2 != n:
        extra_base = 2.0 ** (-(2.0 ** (-(math.log2(2 * cp2) - 3))))
        rem = min(cp2, n - cp2)
        slopes = np.concatenate(
            [slopes, extra_base ** np.arange(1, 1 + 2 * rem, 2, dtype=np.float64)])
    return slopes


SLOPES = _alibi_slopes()


def _head_width(h):
    w = 512
    while SLOPES[h] * w > 64.0:
        w //= 2
    return max(w, 64)


WH = [_head_width(h) for h in range(NH)]

T_SKIP = 200.0


def _n_units(h):
    n = 1
    while n < 17 and SLOPES[h] * (512 * (n - 1) + WH[h]) < T_SKIP:
        n += 1
    return n


NUNIT = [_n_units(h) for h in range(NH)]
UA = [min(NUNIT[h], 8) for h in range(NH)]
UB = [min(NUNIT[h], 16) for h in range(NH)]

_BIAS_COLS = {}
_BIAS_H0 = {}
_nb = 0
for _h in range(NH):
    _BIAS_H0[_h] = _nb
    for _uu in range(UA[_h] + UB[_h]):
        for _t in range(4):
            for _s in range(512 // WH[_h]):
                _BIAS_COLS[(_h, _uu, _t, _s)] = _nb
                _nb += 1
NBIAS = _nb
NBH = {h: (UA[h] + UB[h]) * 4 * (512 // WH[h]) for h in range(NH)}


def _slots_for_core(c):
    a = [(0, c - u, True) if u <= c else (0, 0, False) for u in range(8)]
    b = [(1, 15 - c - u, True) if u <= 15 - c else (1, 0, False)
         for u in range(16)]
    return a + b


def _rfom_matrix():
    n = D
    sqrt_n = math.sqrt(n)
    pair_scale = math.sqrt(n / 2.0)
    max_f = (n - 1) // 2
    c = np.eye(n, dtype=np.float64)
    dc = (c[:, 0] * sqrt_n)[:, None].astype(np.complex128)
    cos = c[:, 1:1 + 2 * max_f:2] * pair_scale
    sin = c[:, 2:2 + 2 * max_f:2] * pair_scale
    mid = (cos - 1j * sin).astype(np.complex128)
    parts = [dc, mid]
    if n % 2 == 0:
        parts.append((c[:, -1] * sqrt_n)[:, None].astype(np.complex128))
    spectrum = np.concatenate(parts, axis=1)
    return np.fft.irfft(spectrum, n=n, axis=1).astype(np.float32)


def _host_tables(c):
    slots = _slots_for_core(c)
    off = np.zeros((1, 80), np.int32)
    for i, (cb, sks, real) in enumerate(slots):
        rank, half = (sks, 0) if sks < NCORE else (15 - sks, 1)
        off[0, i] = (rank * 128) * 1024 + half * 512
        off[0, 24 + i] = (rank * 1024 + half * 512) * 129
    bias = np.full((P, NBIAS), NEG, np.float32)
    p = np.arange(P, dtype=np.float64)
    for h in range(NH):
        w = WH[h]
        sl = SLOPES[h]
        for uu in range(UA[h] + UB[h]):
            i = uu if uu < UA[h] else 8 + (uu - UA[h])
            cb, sks, real = slots[i]
            if not real:
                continue
            ch_sup = c if cb == 0 else 15 - c
            for t in range(4):
                sk = sks * 512 + 128 * t + p
                for s in range(512 // w):
                    sq_hi = 512 * ch_sup + s * w + w
                    bias[:, _BIAS_COLS[(h, uu, t, s)]] = (sl * (sk - sq_hi)).astype(np.float32)
    return off, bias


def _m0_mask():
    m = np.zeros((P, 1024), np.float32)
    jj = np.arange(1024)[None, :]
    pp = np.arange(P)[:, None]
    m[jj < pp + 512] = NEG
    return m


def _mask_delta(t, s, w):
    return 128 * t - w * s


def _is_diag_slot(i):
    return i in (0, 8)


class _AttnPlan:
    def __init__(self):
        self.n_sc_after_h = []
        self.avA = []
        self.avB = []
        n_sc = 0
        n_av = 0
        for h in range(NH):
            ns = 512 // WH[h]
            n_sc += (UA[h] + UB[h]) * 4 * ns
            self.n_sc_after_h.append(n_sc)
            n_av += UA[h] * 4
            self.avA.append(n_av)
            n_av += UB[h] * 4
            self.avB.append(n_av)


PLAN = _AttnPlan()


class _Sem:
    def __init__(self, ctx, nc, name):
        self.h = ctx.enter_context(nc.semaphore(name))

    def inc(self, instr, k=1):
        instr.then_inc(self.h, k)
        return instr


FTOT = dict(Ln=4, Rs=8, Ap=4, Tr=32, Xl=32, E=2, Ev=2, Wt0=128, Wt1=128,
            Bm=64, Be=64, Fin=1)
QTOT = dict(Ln=4, Rs=8, Ap=4, Tr=32, Xl=32, Kw=16, Km=5, Ke=5, Qw0=128, Qw1=128,
            Qm=32, Qe=32, Qs0=256, Qs1=256)


class _CS:

    def __init__(self, sems, tot, idx):
        self.s, self.tot, self.i = sems, tot, idx

    def a(self, k, n):
        return self.tot[k] * self.i + n

    def w(self, eng, k, n):
        eng.wait_ge(self.s[k].h, self.a(k, n))

    def inc(self, k, ins, d=1):
        ins.then_inc(self.s[k].h, d)
        return ins


def build_nc():
    from contextlib import ExitStack
    nc = bass.Bass("TRN2", target_bir_lowering=False)
    nc.has_collectives = True

    def DI(name, shape, dt):
        return nc.dram_tensor(name, shape, dt, kind="ExternalInput")

    xin = DI("xin", [SHARD, D], F32)
    wqkv = DI("wqkv", [D, D + 2 * DH], F32)
    wc = DI("wc", [D, D], F32)
    wbase = DI("wbase", [D, D], F32)
    bbase = DI("bbase", [1, D], F32)
    enc0 = DI("enc0", [32, D], F32)
    enc1 = DI("enc1", [8, D], F32)
    coefs0 = DI("coefs0", [3, D, 32], F32)
    coefs1 = DI("coefs1", [3, D, 8], F32)
    cmask0 = DI("cmask0", [3, D, 32], U8)
    cmask1 = DI("cmask1", [3, D, 8], U8)
    msl = DI("msl", [D, 512], F32)
    biastbl = DI("biastbl", [P, NBIAS], F32)
    m0in = DI("m0in", [P, 1024], F32)
    identin = DI("identin", [P, P], BF16)
    onesin = DI("onesin", [1, P], BF16)
    offin = DI("offin", [1, 80], I32)

    xout = nc.dram_tensor("xout", [SHARD, D], F32, kind="ExternalOutput")

    WT_d = nc.dram_tensor("WT_d", [16 * P, 32 * 256], BF16)
    wcb_d = nc.dram_tensor("wcb_d", [16 * P, 32 * 256], BF16)
    wqb_d = nc.dram_tensor("wqb_d", [D, D + 2 * DH], BF16)
    qt_d = nc.dram_tensor("qt_d", [NH * P, 1024], BF16)
    at_d = nc.dram_tensor("at_d", [P, NH * 512], BF16)
    xpg = nc.dram_tensor("xpg", [SHARD, D], F32)
    g_kin = nc.dram_tensor("g_kin", [P, 1024], BF16)
    g_kout = nc.dram_tensor("g_kout", [NCORE * P, 1024], BF16)
    g_vin = nc.dram_tensor("g_vin", [1024, 129], BF16)
    g_vout = nc.dram_tensor("g_vout", [S, 129], BF16)
    g_gin = nc.dram_tensor("g_gin", [40, 512], BF16)
    g_gout = nc.dram_tensor("g_gout", [NCORE * 40, 512], BF16)

    ctx = ExitStack()
    with ctx:
        def at(name, shape, dt, off):
            return nc.alloc_sbuf_tensor_at(name, shape, dt, offset=off)

        BASE = 16512
        XO, WO, SO = BASE, BASE + 65536, BASE + 131072
        xbuf = at("xbuf", [P, 4 * D], F32, XO)
        ktit = at("ktit", [P, NSLOT * 512], BF16, XO)
        viti = at("viti", [P, NSLOT * 516], BF16, XO + 24576)
        m0m = at("m0m", [P, 1024], F32, XO + 49408)
        btb = [at(f"btb{k}", [P, 768], F32, XO + 53504 + k * 3072)
               for k in range(2)]
        qw2 = at("qw2", [P, 32 * 256], BF16, XO + 32768)
        assert 53504 + 6144 <= 65536

        xlt = at("xlt", [P, 32 * 512], BF16, WO)
        wt0 = at("wt0", [P, 32 * 256], BF16, WO + 32768)
        wt1 = at("wt1", [P, 32 * 256], BF16, WO + 49152)
        coefs_f = at("coefs_f", [P, 3840], F32, WO)
        cmask_f = at("cmask_f", [P, 3840], F32, WO + 15360)
        m_bf = at("m_bf", [P, 32 * 512], BF16, WO + 32768)

        so = [SO]

        def sb(name, shape, dt):
            t = at(name, shape, dt, so[0])
            nbytes = shape[1] * {F32: 4, BF16: 2, U8: 1, I32: 4}[dt]
            so[0] += (nbytes + 63) // 64 * 64
            return t

        g0p = sb("g0p", [32, D], BF16)
        g1p = sb("g1p", [8, D], BF16)
        b_bf = sb("b_bf", [1, D], BF16)
        _enc_off = so[0]
        enc0_bf = sb("enc0_bf", [32, D], BF16)
        enc1_bf = sb("enc1_bf", [8, D], BF16)
        wnat2 = at("wnat2", [P, D], BF16, _enc_off)
        wnat = at("wnat", [P, D], BF16, _enc_off + 8192)
        qb4 = [wnat2[:, 0:2048], wnat2[:, 2048:4096],
               wnat[:, 0:2048], wnat[:, 2048:4096]]
        enc0T = sb("enc0T", [P, 32 * 32], BF16)
        enc1T = sb("enc1T", [P, 32 * 8], BF16)
        e0bf = sb("e0bf", [32, 512], BF16)
        e1bf = sb("e1bf", [8, 512], BF16)
        ktown = sb("ktown", [P, 1024], BF16)
        vown = sb("vown", [P, 8 * 129], BF16)
        ident = sb("ident", [P, P], BF16)
        ones1 = sb("ones1", [1, P], BF16)
        w0s = sb("w0s", [P, 32 * 32], BF16)
        w1s = sb("w1s", [P, 32 * 8], BF16)
        xnst = sb("xnst", [P, D], BF16)
        stats = sb("stats", [P, 48], F32)
        mv = sb("mv", [P, 8], F32)
        lnv = sb("lnv", [P, 4], F32)
        rstd = sb("rstd", [P, 4], F32)
        epst = sb("epst", [P, 1], F32)
        recip = sb("recip", [P, 8], F32)
        offt = sb("offt", [1, 80], I32)
        _pq_off = so[0]
        probT = [sb(f"pt{k}", [P, 512], BF16) for k in range(4)]
        qbuf = [sb(f"qb{k}", [P, 1024], BF16) for k in range(2)]
        anat = sb("anat", [P, 512], BF16)
        atst = sb("atst", [P, 512], BF16)
        gst = sb("gst", [40, 512], BF16)
        kvstg = sb("kvstg", [P, 512], BF16)
        cmask_u = sb("cmask_u", [P, 3840], U8)
        wstgAB = at("wstgAB", [P, 2 * D], BF16, _pq_off)
        so[0] = max(so[0], _pq_off + 16384)
        kvb = [sb("kvb0", [P, 256], BF16), sb("kvb1", [P, 256], BF16)]
        assert so[0] <= 229344, f"S overflow {so[0]}"

        ps4 = nc.alloc_psum_tensor("ps4", [P, 2048], F32)
        ps_sc = nc.alloc_psum_tensor("ps_sc", [P, 1024], F32)
        ps_tr = nc.alloc_psum_tensor("ps_tr", [P, 512], F32)
        ps_x = nc.alloc_psum_tensor("ps_x", [P, 512], F32)

        def slot4(idx, width=256):
            return ps4[:, (idx % 4) * 512:(idx % 4) * 512 + width]

        def sc_slot(k, width):
            r = k % 3
            if r == 0:
                return ps_sc[:, 0:width]
            if r == 1:
                return ps_sc[:, 512:512 + width]
            return ps_x[:, 0:width]

        def mkS(name):
            return _Sem(ctx, nc, name)

        s_ld = mkS("s_ld")
        s_cast = mkS("s_cast")
        s_wsum = mkS("s_wsum")
        s_encT = mkS("s_encT")
        s_encE = mkS("s_encE")
        s_gmm = mkS("s_gmm")
        s_gev = mkS("s_gev")
        s_gst = mkS("s_gst")
        s_cc = mkS("s_cc")
        s_gld = mkS("s_gld")
        s_wpt = mkS("s_wpt")
        s_wpe = mkS("s_wpe")
        s_wps = [mkS("s_wps0"), mkS("s_wps1")]
        s_xpg = mkS("s_xpg")
        s_xld = mkS("s_xld")
        s_kvst = mkS("s_kvst")
        s_wcs = [mkS("s_wcs0"), mkS("s_wcs1")]
        s_wcc = [mkS("s_wcc0"), mkS("s_wcc1")]
        s_qpc = mkS("s_qpc")
        s_qpk = mkS("s_qpk")
        s_qps = [mkS("s_qps0"), mkS("s_qps1")]
        s_tbl = mkS("s_tbl")
        s_kv = mkS("s_kv")
        s_kvg = mkS("s_kvg")
        s_bt = [mkS("s_bt0"), mkS("s_bt1")]
        s_wc = [mkS("s_wc0"), mkS("s_wc1")]
        s_q = [mkS("s_q0"), mkS("s_q1")]
        t_sc = mkS("t_sc")
        a_ex = mkS("a_ex")
        v_mk = mkS("v_mk")
        t_av = mkS("t_av")
        v_rc = mkS("v_rc")
        v_an = mkS("v_an")
        t_tr = mkS("t_tr")
        v_at = mkS("v_at")
        s_at = mkS("s_at")
        s_x5 = mkS("s_x5")
        s_a5 = mkS("s_a5")
        s_w5 = [mkS("s_w50"), mkS("s_w51")]
        t_op = mkS("t_op")
        v_op = mkS("v_op")
        s_out = mkS("s_out")
        vch = mkS("vch")
        VCH = [0]

        def vedge(vector, ins):
            VCH[0] += 1
            ins.then_inc(vch.h, 1)
            vector.wait_ge(vch.h, VCH[0])

        fS = {k: mkS(f"f{k}") for k in FTOT}
        qS = {k: mkS(f"q{k}") for k in QTOT}
        fA, fB, f2A, f2B = (_CS(fS, FTOT, i) for i in range(4))
        qB, qA = (_CS(qS, QTOT, i) for i in range(2))

        block = ctx.enter_context(nc.Block())

        def ln_vec(vector, q, waits, st_waits=None):
            for hh, val in waits:
                vector.wait_ge(hh, val)
            for st in range(4):
                if st > 0:
                    q.w(vector, "Ln", st)
                for hh, val in (st_waits or {}).get(st, ()):
                    vector.wait_ge(hh, val)
                for k in range(8):
                    ins = vector.bn_stats(
                        out=stats[:, 6 * k:6 * k + 6],
                        in_=xbuf[:, st * D + 512 * k:st * D + 512 * (k + 1)])
                vedge(vector, ins)
                q.inc("Ln", vector.bn_aggr(out=mv[:, 2 * st:2 * st + 2], in_=stats[:]))
            for st in range(4):
                q.w(vector, "Rs", 2 * (st + 1))
                if st > 0:
                    q.w(vector, "Tr", 8 * st)
                q.inc("Ap", vector.tensor_scalar(
                    out=xnst[:], in0=xbuf[:, st * D:(st + 1) * D],
                    scalar1=mv[:, 2 * st:2 * st + 1], scalar2=rstd[:, st:st + 1],
                    op0=ALU.subtract, op1=ALU.mult))
                for g in range(8):
                    gi = st * 8 + g
                    q.w(vector, "Tr", gi + 1)
                    dst = xlt[:].rearrange("p (dt s) -> p dt s", s=512)[
                        :, g * 4:(g + 1) * 4, st * 128:(st + 1) * 128]
                    src = (ps_tr if g % 2 == 0 else ps_x)[:].rearrange(
                        "p (k c) -> p k c", c=128)
                    q.inc("Xl", vector.tensor_copy(out=dst, in_=src))

        def ln_scal(scalar, q):
            for st in range(4):
                q.w(scalar, "Ln", st + 1)
                q.inc("Rs", scalar.activation(out=lnv[:, st:st + 1],
                                              in_=mv[:, 2 * st + 1:2 * st + 2],
                                              func=AF.Ln, bias=epst[:, 0:1], scale=1.0))
                q.w(scalar, "Rs", 2 * st + 1)
                q.inc("Rs", scalar.activation(out=rstd[:, st:st + 1],
                                              in_=lnv[:, st:st + 1],
                                              func=AF.Exp, bias=0.0, scale=-0.5))

        def ln_tens(tensor, q):
            for st in range(4):
                q.w(tensor, "Ap", st + 1)
                for g in range(8):
                    gi = st * 8 + g
                    pslot = ps_tr if g % 2 == 0 else ps_x
                    if gi >= 2:
                        q.w(tensor, "Xl", gi - 1)
                    for k in range(4):
                        dsl = g * 4 + k
                        ins = tensor.matmul(out=pslot[:, k * 128:(k + 1) * 128],
                                            lhsT=xnst[:, dsl * 128:(dsl + 1) * 128],
                                            rhs=ident[:], start=True, stop=True)
                    q.inc("Tr", ins)

        def ffn_tens(tensor, q, prep, gl_wait, pre=()):
            for hh, val in pre:
                tensor.wait_ge(hh, val)
            ln_tens(tensor, q)
            q.w(tensor, "Xl", 32)
            for dt in range(32):
                ins = tensor.matmul(out=ps_x[0:32, 0:512],
                                    lhsT=enc0T[:, dt * 32:(dt + 1) * 32],
                                    rhs=xlt[:, dt * 512:(dt + 1) * 512],
                                    start=(dt == 0), stop=(dt == 31))
            q.inc("E", ins)
            for dt in range(32):
                ins = tensor.matmul(out=ps_tr[0:8, 0:512],
                                    lhsT=enc1T[:, dt * 8:(dt + 1) * 8],
                                    rhs=xlt[:, dt * 512:(dt + 1) * 512],
                                    start=(dt == 0), stop=(dt == 31))
            q.inc("E", ins)
            q.w(tensor, "Ev", 2)
            if gl_wait:
                tensor.wait_ge(s_gld.h, 32)
            if prep:
                for ot in range(6):
                    wprep_tens(tensor, ot)
            for os_ in range(16):
                if prep:
                    for ot in (2 * os_ + 6, 2 * os_ + 7):
                        if ot < 32:
                            wprep_tens(tensor, ot)
                q.w(tensor, f"Wt{os_ % 2}", 16 * (os_ // 2 + 1))
                wts = wt0 if os_ % 2 == 0 else wt1
                for st in range(4):
                    idx = os_ * 4 + st
                    if idx >= 4:
                        q.w(tensor, "Be", idx - 3)
                    pa = slot4(idx)
                    tensor.matmul(out=pa, lhsT=ones1[:],
                                  rhs=b_bf[0:1, os_ * 256:(os_ + 1) * 256],
                                  start=True, stop=False)
                    for dt in range(32):
                        tensor.matmul(out=pa,
                                      lhsT=xlt[:, dt * 512 + st * 128:dt * 512 + (st + 1) * 128],
                                      rhs=wts[:, dt * 256:(dt + 1) * 256],
                                      start=False, stop=False)
                    tensor.matmul(out=pa, lhsT=e0bf[:, st * 128:(st + 1) * 128],
                                  rhs=g0p[:, os_ * 256:(os_ + 1) * 256],
                                  start=False, stop=False)
                    q.inc("Bm", tensor.matmul(
                        out=pa, lhsT=e1bf[:, st * 128:(st + 1) * 128],
                        rhs=g1p[:, os_ * 256:(os_ + 1) * 256],
                        start=False, stop=True))

        def ffn_vec(vector, q, waits, prep):
            ln_vec(vector, q, waits)
            q.w(vector, "E", 1)
            q.inc("Ev", vector.tensor_copy(out=e0bf[:], in_=ps_x[0:32, 0:512]))
            q.w(vector, "E", 2)
            q.inc("Ev", vector.tensor_copy(out=e1bf[:], in_=ps_tr[0:8, 0:512]))
            if prep:
                for ot in range(6):
                    wprep_vec(vector, ot)
            for os_ in range(16):
                if prep:
                    for ot in (2 * os_ + 6, 2 * os_ + 7):
                        if ot < 32:
                            wprep_vec(vector, ot)
                for st in range(4):
                    idx = os_ * 4 + st
                    q.w(vector, "Bm", idx + 1)
                    q.inc("Be", vector.scalar_tensor_tensor(
                        out=xbuf[:, st * D + os_ * 256:st * D + (os_ + 1) * 256],
                        in0=slot4(idx), scalar=0.5,
                        in1=xbuf[:, st * D + os_ * 256:st * D + (os_ + 1) * 256],
                        op0=ALU.mult, op1=ALU.add))
            q.inc("Fin", vector.tensor_copy(out=epst[:], in_=epst[:]))

        def ffn_sync(sync, q, prep):
            if prep:
                sync.wait_ge(s_gmm.h, 2)
                wprep_sync(sync, 0)
            for os_ in range(16):
                if prep:
                    sync.wait_ge(s_wps[0].h, 16 * (os_ + 1))
                else:
                    sync.wait_ge(s_wps[0].h, 256)
                if os_ >= 2:
                    q.w(sync, "Bm", 4 * (os_ - 1))
                q.inc(f"Wt{os_ % 2}", sync.dma_start(
                    out=(wt0 if os_ % 2 == 0 else wt1)[:],
                    in_=WT_d.ap()[os_ * P:(os_ + 1) * P, :]), 16)
                if prep and os_ + 1 < 16:
                    wprep_sync(sync, os_ + 1)

        def wprep_tens(tensor, ot):
            tensor.wait_ge(s_wc[ot % 2].h, 16 * (ot // 2 + 1))
            src = wnat2 if ot % 2 == 0 else wnat
            for g in range(8):
                gi = ot * 8 + g
                pslot = ps_tr if g % 2 == 0 else ps_x
                if gi >= 2:
                    tensor.wait_ge(s_wpe.h, gi - 1)
                for k in range(4):
                    dsl = g * 4 + k
                    ins = tensor.matmul(out=pslot[:, k * 128:(k + 1) * 128],
                                        lhsT=src[:, dsl * 128:(dsl + 1) * 128],
                                        rhs=ident[:], start=True, stop=True)
                s_wpt.inc(ins)

        def wprep_vec(vector, ot):
            if ot == 0:
                vector.wait_ge(s_gst.h, 16)
            if ot >= 2:
                vector.wait_ge(s_wps[0].h, 16 * (ot // 2))
            half = ot % 2
            for g in range(8):
                gi = ot * 8 + g
                pslot = ps_tr if g % 2 == 0 else ps_x
                vector.wait_ge(s_wpt.h, gi + 1)
                dst = wstgAB[:].rearrange("p (ds pair) -> p ds pair", pair=256)[
                    :, g * 4:(g + 1) * 4, half * 128:half * 128 + 128]
                s_wpe.inc(vector.tensor_copy(
                    out=dst, in_=pslot[:].rearrange("p (k c) -> p k c", c=128)))

        def wprep_sync(sync, os_):
            sync.wait_ge(s_wpe.h, 16 * (os_ + 1))
            s_wps[0].inc(sync.dma_start(
                out=WT_d.ap()[os_ * P:(os_ + 1) * P, :], in_=wstgAB[:]), 16)

        def q_store(eng, j):
            dt = j % 32
            half = j // 32
            eng.wait_ge(s_qpc.h, 512 + 16 * (j + 1))
            s_qps[half].inc(eng.dma_start(
                out=wqb_d.ap()[dt * 128:(dt + 1) * 128,
                               half * 2048:half * 2048 + 2048],
                in_=qb4[j % 4]), 16)

        def qkv_tens(tensor, q, cb, pre=()):
            for hh, val in pre:
                tensor.wait_ge(hh, val)
            ln_tens(tensor, q)
            q.w(tensor, "Xl", 32)
            q.w(tensor, "Kw", 16)
            for dt in range(32):
                ins = tensor.matmul(out=ps_x[:, 0:512],
                                    lhsT=wstgAB[:, dt * 256:dt * 256 + 128],
                                    rhs=xlt[:, dt * 512:(dt + 1) * 512],
                                    start=(dt == 0), stop=(dt == 31))
            q.inc("Km", ins)
            for st in range(4):
                for dt in range(32):
                    ins = tensor.matmul(
                        out=ps_tr[:, st * 128:(st + 1) * 128],
                        lhsT=xlt[:, dt * 512 + st * 128:dt * 512 + (st + 1) * 128],
                        rhs=wstgAB[:, dt * 256 + 128:dt * 256 + 256],
                        start=(dt == 0), stop=(dt == 31))
                q.inc("Km", ins)
            for h in range(NH):
                hp = h // 2
                q.w(tensor, f"Qw{hp % 2}", 16 * (hp // 2 + 1))
                if h >= 2:
                    q.w(tensor, "Qe", h - 1)
                wsl = wt0 if hp % 2 == 0 else qw2
                for dt in range(32):
                    ins = tensor.matmul(
                        out=ps_sc[:, (h % 2) * 512:(h % 2) * 512 + 512],
                        lhsT=wsl[:, dt * 256 + (h % 2) * 128:dt * 256 + (h % 2) * 128 + 128],
                        rhs=xlt[:, dt * 512:(dt + 1) * 512],
                        start=(dt == 0), stop=(dt == 31))
                q.inc("Qm", ins)

        def qkv_vec(vector, q, cb, waits, st_waits=None):
            ln_vec(vector, q, waits, st_waits)
            q.w(vector, "Km", 1)
            q.inc("Ke", vector.tensor_copy(out=ktown[:, cb * 512:(cb + 1) * 512],
                                           in_=ps_x[:, 0:512]))
            q.w(vector, "Km", 5)
            for st in range(4):
                q.inc("Ke", vector.tensor_copy(
                    out=vown[:, (cb * 4 + st) * 129:(cb * 4 + st) * 129 + 128],
                    in_=ps_tr[:, st * 128:(st + 1) * 128]))
            for h in range(NH):
                q.w(vector, "Qm", h + 1)
                if h >= 2:
                    q.w(vector, f"Qs{h % 2}", 16 * (h // 2))
                elif q.i > 0:
                    vector.wait_ge(q.s[f"Qs{h % 2}"].h, q.tot[f"Qs{h % 2}"] * q.i)
                q.inc("Qe", vector.tensor_copy(
                    out=qbuf[h % 2][:, 0:512],
                    in_=ps_sc[:, (h % 2) * 512:(h % 2) * 512 + 512]))

        def qkv_sync(sync, q, cb, kw_pre, hp_pre, mid=None):
            sync.wait_ge(s_qpk.h, 512)
            sync.wait_ge(s_wps[0].h, 256)
            for hh, vv in kw_pre:
                sync.wait_ge(hh, vv)
            q.inc("Kw", sync.dma_start(
                out=wstgAB[:].rearrange("p (dt c) -> p dt c", c=256),
                in_=wqb_d.ap()[:, D:D + 256].rearrange(
                    "(dt p) c -> p dt c", p=P)), 16)
            sync.wait_ge(s_qps[0].h, 512)
            for hh, vv in hp_pre:
                sync.wait_ge(hh, vv)
            for hp in range(16):
                if hp == 8:
                    if mid is not None:
                        mid()
                    sync.wait_ge(s_qps[1].h, 512)
                if hp >= 2:
                    q.w(sync, "Qm", 2 * hp - 2)
                q.inc(f"Qw{hp % 2}", sync.dma_start(
                    out=(wt0 if hp % 2 == 0 else qw2)[:].rearrange(
                        "p (dt c) -> p dt c", c=256),
                    in_=wqb_d.ap()[:, hp * 256:(hp + 1) * 256].rearrange(
                        "(dt p) c -> p dt c", p=P)), 16)
                for h in (2 * hp - 4, 2 * hp - 3):
                    if h >= 0:
                        q.w(sync, "Qe", h + 1)
                        q.inc(f"Qs{h % 2}", sync.dma_start(
                            out=qt_d.ap()[h * P:(h + 1) * P,
                                          cb * 512:(cb + 1) * 512],
                            in_=qbuf[h % 2][:, 0:512]), 16)
            for h in (28, 29, 30, 31):
                q.w(sync, "Qe", h + 1)
                q.inc(f"Qs{h % 2}", sync.dma_start(
                    out=qt_d.ap()[h * P:(h + 1) * P, cb * 512:(cb + 1) * 512],
                    in_=qbuf[h % 2][:, 0:512]), 16)

        @block.sync
        def _(sync):
            loads = [
                (ident[:], identin.ap()),
                (ones1[:], onesin.ap()),
                (offt[:], offin.ap()),
                (xbuf[:].rearrange("p (st d) -> p st d", d=D),
                 xin.ap()[0:512, :].rearrange("(st p) d -> p st d", p=P)),
                (coefs_f[:, 0:3072].rearrange("p (eo r) -> p eo r", r=32),
                 coefs0.ap().rearrange("e (ot p) r -> p (e ot) r", p=P)),
                (coefs_f[:, 3072:3840].rearrange("p (eo r) -> p eo r", r=8),
                 coefs1.ap().rearrange("e (ot p) r -> p (e ot) r", p=P)),
                (cmask_u[:, 0:3072].rearrange("p (eo r) -> p eo r", r=32),
                 cmask0.ap().rearrange("e (ot p) r -> p (e ot) r", p=P)),
                (cmask_u[:, 3072:3840].rearrange("p (eo r) -> p eo r", r=8),
                 cmask1.ap().rearrange("e (ot p) r -> p (e ot) r", p=P)),
            ]
            for dst, src in loads:
                s_ld.inc(sync.dma_start(out=dst, in_=src), 16)
            sync.wait_ge(s_gev.h, 2)
            s_gst.inc(sync.dma_start(out=g_gin.ap(), in_=gst[:]), 16)
            sync.wait_ge(s_cc.h, 1)
            s_gld.inc(sync.dma_start(
                out=g0p[:].rearrange("i (r o) -> i r o", o=512),
                in_=g_gout.ap().rearrange("(r i) o -> i r o", i=40)[0:32, :, :]), 16)
            s_gld.inc(sync.dma_start(
                out=g1p[:].rearrange("i (r o) -> i r o", o=512),
                in_=g_gout.ap().rearrange("(r i) o -> i r o", i=40)[32:40, :, :]), 16)
            ffn_sync(sync, fA, prep=True)

            def mid_half1():
                for j in range(32, 64):
                    q_store(sync, j)

            fA.w(sync, "Be", 64)
            s_xpg.inc(sync.dma_start(
                out=xpg.ap()[0:512, :].rearrange("(st p) d -> p st d", p=P),
                in_=xbuf[:].rearrange("p (st d) -> p st d", d=D)), 16)
            sync.wait_ge(s_xpg.h, 16)
            s_xld.inc(sync.dma_start(
                out=xbuf[:].rearrange("p (st d) -> p st d", d=D),
                in_=xin.ap()[512:1024, :].rearrange("(st p) d -> p st d", p=P)), 16)
            ffn_sync(sync, fB, prep=False)
            qB.w(sync, "Ap", 4)
            qB.w(sync, "Xl", 32)
            s_xpg.inc(sync.dma_start(
                out=xpg.ap()[512:1024, :].rearrange("(st p) d -> p st d", p=P),
                in_=xbuf[:].rearrange("p (st d) -> p st d", d=D)), 16)
            sync.wait_ge(s_xpg.h, 32)
            s_xld.inc(sync.dma_start(
                out=xbuf[:].rearrange("p (st d) -> p st d", d=D)[:, 0:2, :],
                in_=xpg.ap()[0:256, :].rearrange("(st p) d -> p st d", p=P)), 16)
            s_xld.inc(sync.dma_start(
                out=xbuf[:, 3 * D:4 * D], in_=xpg.ap()[384:512, :]), 16)
            qkv_sync(sync, qB, 1, kw_pre=[],
                     hp_pre=[(fS["Bm"].h, fB.a("Bm", 64)), (s_xpg.h, 32)],
                     mid=mid_half1)
            qB.w(sync, "Qm", 32)
            s_xld.inc(sync.dma_start(
                out=xbuf[:, 2 * D:3 * D], in_=xpg.ap()[256:384, :]), 16)
            qkv_sync(sync, qA, 0,
                     kw_pre=[(qS["Km"].h, qB.a("Km", 5)),
                             (qS["Qs0"].h, qB.a("Qs0", 256)),
                             (qS["Qs1"].h, qB.a("Qs1", 256))],
                     hp_pre=[(qS["Qm"].h, qB.a("Qm", 32)),
                             (qS["Ap"].h, qA.a("Ap", 4))])
            qA.w(sync, "Ke", 5)
            qB.w(sync, "Ke", 5)
            s_kvst.inc(sync.dma_start(out=g_kin.ap(), in_=ktown[:]), 16)
            s_kvst.inc(sync.dma_start(
                out=g_vin.ap().rearrange("(st p) v -> p st v", p=P),
                in_=vown[:].rearrange("p (st v) -> p st v", v=129)), 16)
            qA.w(sync, "Ap", 4)
            s_tbl.inc(sync.dma_start(out=m0m[:], in_=m0in.ap()), 16)
            sync.wait_ge(s_cc.h, 2)
            qA.w(sync, "Qm", 32)
            with sync.register("r_ko") as r_ko:
                for i in range(12):
                    sync.reg_load(r_ko, offt[0:1, i:i + 1])
                    s_kv.inc(sync.dma_start(
                        out=ktit[:, i * 512:(i + 1) * 512],
                        in_=bass.AP(g_kout, r_ko, [[1024, P], [1, 512]])), 16)
                sync.wait_ge(s_cc.h, 3)
                for i in range(12, 18):
                    sync.reg_load(r_ko, offt[0:1, 24 + i:25 + i])
                    s_kv.inc(sync.dma_start(
                        out=viti[:, i * 516:(i + 1) * 516].rearrange(
                            "p (t v) -> p t v", v=129),
                        in_=bass.AP(g_vout, r_ko, [[129, P], [129 * P, 4], [1, 129]])), 16)
            qA.w(sync, "Qs0", 256)
            qA.w(sync, "Qs1", 256)
            qB.w(sync, "Qs0", 256)
            qB.w(sync, "Qs1", 256)
            for h in range(NH):
                if h >= 2:
                    sync.wait_ge(t_sc.h, PLAN.n_sc_after_h[h - 2] + 1)
                s_q[h % 2].inc(sync.dma_start(
                    out=qbuf[h % 2][:],
                    in_=qt_d.ap()[h * P:(h + 1) * P, :]), 16)
                s_bt[h % 2].inc(sync.dma_start(
                    out=btb[h % 2][:, 0:NBH[h]],
                    in_=biastbl.ap()[:, _BIAS_H0[h]:_BIAS_H0[h] + NBH[h]]), 16)
                if h >= 1:
                    sync.wait_ge(v_at.h, 2 * (h - 1) + 2)
                    s_at.inc(sync.dma_start(
                        out=at_d.ap()[:, (h - 1) * 512:h * 512],
                        in_=atst[:]), 16)
            sync.wait_ge(v_at.h, 64)
            s_at.inc(sync.dma_start(
                out=at_d.ap()[:, 31 * 512:32 * 512], in_=atst[:]), 16)
            sync.wait_ge(t_av.h, PLAN.avB[NH - 1])
            sync.wait_ge(s_wcs[0].h, 256)
            sync.wait_ge(s_wcs[1].h, 256)
            for es in range(16):
                if es >= 2:
                    sync.wait_ge(t_op.h, 4 * (es - 1))
                s_w5[es % 2].inc(sync.dma_start(
                    out=(wt0 if es % 2 == 0 else wt1)[:],
                    in_=wcb_d.ap()[es * P:(es + 1) * P, :]), 16)
            sync.wait_ge(t_op.h, 64)
            ffn_sync(sync, f2A, prep=False)
            f2A.w(sync, "Be", 64)
            s_out.inc(sync.dma_start(
                out=xout.ap()[0:512, :].rearrange("(st p) d -> p st d", p=P),
                in_=xbuf[:].rearrange("p (st d) -> p st d", d=D)), 16)
            for es in range(16):
                gi = 16 + es
                if gi >= 18:
                    sync.wait_ge(t_op.h, 4 * (gi - 1))
                s_w5[gi % 2].inc(sync.dma_start(
                    out=(wt0 if gi % 2 == 0 else wt1)[:],
                    in_=wcb_d.ap()[es * P:(es + 1) * P, :]), 16)
            sync.wait_ge(t_op.h, 128)
            ffn_sync(sync, f2B, prep=False)
            f2B.w(sync, "Be", 64)
            s_out.inc(sync.dma_start(
                out=xout.ap()[512:1024, :].rearrange("(st p) d -> p st d", p=P),
                in_=xbuf[:].rearrange("p (st d) -> p st d", d=D)), 16)
            sync.wait_ge(s_out.h, 32)

        @block.gpsimd
        def _(gpsimd):
            s_cast.inc(gpsimd.dma_start(out=enc0_bf[:], in_=enc0.ap()), 16)
            s_cast.inc(gpsimd.dma_start(out=enc1_bf[:], in_=enc1.ap()), 16)
            s_cast.inc(gpsimd.dma_start(out=b_bf[:], in_=bbase.ap()), 16)
            s_cast.inc(gpsimd.dma_start(
                out=m_bf[:].rearrange("p (dt o) -> p dt o", o=512),
                in_=msl.ap().rearrange("(dt p) o -> p dt o", p=P)), 16)
            gpsimd.wait_ge(s_gst.h, 16)
            s_cc.inc(gpsimd.collective_compute(
                "AllGather", ALU.bypass, replica_groups=[list(range(NCORE))],
                ins=[g_gin.ap()], outs=[g_gout.ap()]))
            gpsimd.wait_ge(s_encT.h, 3)
            for ot in range(32):
                if ot >= 2:
                    gpsimd.wait_ge(s_wpt.h, 8 * (ot - 1))
                s_wc[ot % 2].inc(gpsimd.dma_start(
                    out=(wnat2 if ot % 2 == 0 else wnat)[:],
                    in_=wbase.ap()[ot * 128:(ot + 1) * 128, :]), 16)
            for k in range(32):
                if k >= 2:
                    gpsimd.wait_ge(s_qpk.h, 16 * (k - 1))
                s_qpc.inc(gpsimd.dma_start(
                    out=kvb[k % 2][:],
                    in_=wqkv.ap()[k * 128:(k + 1) * 128, D:D + 256]), 16)
            gpsimd.wait_ge(s_wpt.h, 256)
            for j in range(64):
                dt = j % 32
                half = j // 32
                pj = j - 4
                if pj >= 0:
                    if pj < 32:
                        gpsimd.wait_ge(s_qps[0].h, 16 * (pj + 1))
                    else:
                        gpsimd.wait_ge(s_qps[1].h, 16 * (pj - 31))
                s_qpc.inc(gpsimd.dma_start(
                    out=qb4[j % 4],
                    in_=wqkv.ap()[dt * 128:(dt + 1) * 128,
                                  half * 2048:half * 2048 + 2048]), 16)
            gpsimd.wait_ge(s_qps[0].h, 512)
            gpsimd.wait_ge(s_qps[1].h, 512)
            for ot in range(32):
                if ot >= 2:
                    gpsimd.wait_ge(s_wcs[ot % 2].h, 16 * (ot // 2))
                s_wcc[ot % 2].inc(gpsimd.dma_start(
                    out=(wnat2 if ot % 2 == 0 else wnat)[:],
                    in_=wc.ap()[ot * 128:(ot + 1) * 128, :]), 16)
                gpsimd.wait_ge(s_wcc[ot % 2].h, 16 * (ot // 2 + 1))
                dst = wcb_d.ap().rearrange(
                    "(es p) (ht e) -> p es ht e", p=P, e=256)[:, :, ot, :]
                src = (wnat2 if ot % 2 == 0 else wnat)[:].rearrange(
                    "p (es e) -> p es e", e=256)
                s_wcs[ot % 2].inc(gpsimd.dma_start(out=dst, in_=src), 16)
            gpsimd.wait_ge(s_kvst.h, 32)
            s_cc.inc(gpsimd.collective_compute(
                "AllGather", ALU.bypass, replica_groups=[list(range(NCORE))],
                ins=[g_kin.ap()], outs=[g_kout.ap()]))
            s_cc.inc(gpsimd.collective_compute(
                "AllGather", ALU.bypass, replica_groups=[list(range(NCORE))],
                ins=[g_vin.ap()], outs=[g_vout.ap()]))
            gpsimd.wait_ge(s_cc.h, 3)
            gpsimd.wait_ge(qS["Ap"].h, qA.a("Ap", 4))
            gpsimd.wait_ge(qS["Qm"].h, qA.a("Qm", 32))
            with gpsimd.register("r_vg") as r_vg:
                for i in range(12):
                    gpsimd.reg_load(r_vg, offt[0:1, 24 + i:25 + i])
                    s_kvg.inc(gpsimd.dma_start(
                        out=viti[:, i * 516:(i + 1) * 516].rearrange(
                            "p (t v) -> p t v", v=129),
                        in_=bass.AP(g_vout, r_vg, [[129, P], [129 * P, 4], [1, 129]])), 16)

        @block.vector
        def _(vector):
            vector.memset(epst[:], 1e-5)
            for st in range(8):
                vector.memset(vown[:, st * 129 + 128:st * 129 + 129], 1.0)
            vector.wait_ge(s_ld.h, 128)
            vedge(vector, vector.tensor_copy(out=cmask_f[:], in_=cmask_u[:]))
            vedge(vector, vector.tensor_tensor(out=cmask_f[:], in0=cmask_f[:],
                                               in1=coefs_f[:], op=ALU.mult))
            vedge(vector, vector.tensor_tensor(
                out=cmask_f[:, 0:1024], in0=cmask_f[:, 0:1024],
                in1=cmask_f[:, 1024:2048], op=ALU.add))
            vedge(vector, vector.tensor_tensor(
                out=cmask_f[:, 0:1024], in0=cmask_f[:, 0:1024],
                in1=cmask_f[:, 2048:3072], op=ALU.add))
            vedge(vector, vector.tensor_tensor(
                out=cmask_f[:, 3072:3328], in0=cmask_f[:, 3072:3328],
                in1=cmask_f[:, 3328:3584], op=ALU.add))
            vedge(vector, vector.tensor_tensor(
                out=cmask_f[:, 3072:3328], in0=cmask_f[:, 3072:3328],
                in1=cmask_f[:, 3584:3840], op=ALU.add))
            vedge(vector, vector.tensor_copy(out=w0s[:], in_=cmask_f[:, 0:1024]))
            s_wsum.inc(vector.tensor_copy(out=w1s[:], in_=cmask_f[:, 3072:3328]))
            vector.wait_ge(s_encT.h, 1)
            s_encE.inc(vector.tensor_copy(out=enc0T[:, 0:512], in_=ps_tr[:]))
            vector.wait_ge(s_encT.h, 2)
            s_encE.inc(vector.tensor_copy(out=enc0T[:, 512:1024], in_=ps_x[:]))
            vector.wait_ge(s_encT.h, 3)
            s_encE.inc(vector.tensor_copy(out=enc1T[:], in_=ps_tr[:, 0:256]))
            vector.wait_ge(s_gmm.h, 1)
            s_gev.inc(vector.tensor_scalar(out=gst[0:32, :], in0=ps_x[0:32, 0:512],
                                           scalar1=0.5, scalar2=None, op0=ALU.mult))
            vector.wait_ge(s_gmm.h, 2)
            s_gev.inc(vector.tensor_scalar(out=gst[32:40, :], in0=ps_tr[0:8, 0:512],
                                           scalar1=0.5, scalar2=None, op0=ALU.mult))
            ffn_vec(vector, fA, waits=[(s_ld.h, 64)], prep=True)
            ffn_vec(vector, fB, waits=[(s_xld.h, 16)], prep=False)
            qkv_vec(vector, qB, 1, waits=[(fS["Fin"].h, fB.a("Fin", 1)),
                                           (s_wps[0].h, 256)])
            qkv_vec(vector, qA, 0, waits=[(s_xld.h, 48)],
                    st_waits={2: ((s_xld.h, 64),)})
            vector.wait_ge(s_tbl.h, 16)
            n_sc = 0
            n_mk = 0
            n_av = 0
            def attn_evac_front(vector, h, cb):
                idx = h * 2 + cb
                vector.wait_ge(t_av.h, PLAN.avA[h] if cb == 0 else PLAN.avB[h])
                if idx > 0:
                    vector.wait_ge(v_an.h, idx)
                for j in range(4):
                    v_rc.inc(vector.reciprocal(
                        out=recip[:, j:j + 1],
                        in_=ps4[:, j * 512 + 128:j * 512 + 129]))
                vector.wait_ge(v_rc.h, 4 * (idx + 1))
                if idx > 0:
                    vector.wait_ge(t_tr.h, idx)
                for j in range(4):
                    ins = vector.tensor_scalar(
                        out=anat[:, j * 128:(j + 1) * 128],
                        in0=ps4[:, j * 512:j * 512 + 128],
                        scalar1=recip[:, j:j + 1], scalar2=None, op0=ALU.mult)
                v_an.inc(ins)

            def attn_evac_back(vector, h, cb):
                idx = h * 2 + cb
                vector.wait_ge(t_tr.h, idx + 1)
                if cb == 0:
                    v_at.inc(vector.tensor_copy(
                        out=xlt[:, h * 512:(h + 1) * 512], in_=ps_tr[:]))
                else:
                    if h > 0:
                        vector.wait_ge(s_at.h, 16 * h)
                    v_at.inc(vector.tensor_copy(out=atst[:], in_=ps_tr[:]))

            for h in range(NH):
                w = WH[h]
                for cb in range(2):
                    if cb == 1:
                        attn_evac_front(vector, h, 0)
                        attn_evac_back(vector, h, 0)
                    for u in range(UA[h] if cb == 0 else UB[h]):
                        for t in range(4):
                            for s in range(512 // w):
                                slot = sc_slot(n_sc, w)
                                n_sc += 1
                                if u == 0 and (_mask_delta(t, s, w) > -128):
                                    vector.wait_ge(t_sc.h, n_sc)
                                    d = _mask_delta(t, s, w)
                                    v_mk.inc(vector.tensor_tensor(
                                        out=slot, in0=slot,
                                        in1=m0m[:, 512 - d:512 - d + w], op=ALU.add))
                            n_av += 1
                attn_evac_front(vector, h, 1)
                attn_evac_back(vector, h, 1)
            for cb in range(2):
                vector.wait_ge(s_x5.h, 16 * (cb + 1))
                for es in range(16):
                    for st in range(4):
                        gi = cb * 64 + es * 4 + st
                        vector.wait_ge(t_op.h, gi + 1)
                        v_op.inc(vector.tensor_tensor(
                            out=xbuf[:, st * D + es * 256:st * D + (es + 1) * 256],
                            in0=slot4(gi),
                            in1=xbuf[:, st * D + es * 256:st * D + (es + 1) * 256],
                            op=ALU.add))
                v_op.inc(vector.tensor_copy(out=epst[:], in_=epst[:]))
                if cb == 0:
                    ffn_vec(vector, f2A, waits=[(v_op.h, 65)], prep=False)
                else:
                    ffn_vec(vector, f2B, waits=[(v_op.h, 130)], prep=False)

        @block.scalar
        def _(scalar):
            ln_scal(scalar, fA)
            ln_scal(scalar, fB)
            for k in range(32):
                scalar.wait_ge(s_qpc.h, 16 * (k + 1))
                s_qpk.inc(scalar.dma_start(
                    out=wqb_d.ap()[k * 128:(k + 1) * 128, D:D + 256],
                    in_=kvb[k % 2][:]), 16)
            for j in range(32):
                q_store(scalar, j)
            ln_scal(scalar, qB)
            ln_scal(scalar, qA)
            scalar.wait_ge(s_cc.h, 2)
            scalar.wait_ge(qS["Ap"].h, qA.a("Ap", 4))
            scalar.wait_ge(qS["Qm"].h, qA.a("Qm", 32))
            with scalar.register("r_ks") as r_ks:
                for i in range(12, 24):
                    scalar.reg_load(r_ks, offt[0:1, i:i + 1])
                    s_kv.inc(scalar.dma_start(
                        out=ktit[:, i * 512:(i + 1) * 512],
                        in_=bass.AP(g_kout, r_ks, [[1024, P], [1, 512]])), 16)
                scalar.wait_ge(s_cc.h, 3)
                for i in range(18, 24):
                    scalar.reg_load(r_ks, offt[0:1, 24 + i:25 + i])
                    s_kv.inc(scalar.dma_start(
                        out=viti[:, i * 516:(i + 1) * 516].rearrange(
                            "p (t v) -> p t v", v=129),
                        in_=bass.AP(g_vout, r_ks, [[129, P], [129 * P, 4], [1, 129]])), 16)
            scalar.wait_ge(s_tbl.h, 16)
            scalar.wait_ge(s_kv.h, 16 * 36)
            scalar.wait_ge(s_kvg.h, 16 * 12)
            n_sc = 0
            n_mk = 0
            n_av = 0
            buf_last_use = {t: 0 for t in range(4)}
            for h in range(NH):
                w = WH[h]
                ns = 512 // w
                scalar.wait_ge(s_bt[h % 2].h, 16 * (h // 2 + 1))
                uu = 0
                for cb in range(2):
                    for u in range(UA[h] if cb == 0 else UB[h]):
                        for t in range(4):
                            for s in range(ns):
                                slot = sc_slot(n_sc, w)
                                n_sc += 1
                                dm = (u == 0) and (_mask_delta(t, s, w) > -128)
                                if dm:
                                    n_mk += 1
                                    scalar.wait_ge(v_mk.h, n_mk)
                                else:
                                    scalar.wait_ge(t_sc.h, n_sc)
                                if s == 0 and buf_last_use[t] > 0:
                                    scalar.wait_ge(t_av.h, buf_last_use[t])
                                col = (uu * 4 + t) * ns + s
                                a_ex.inc(scalar.activation(
                                    out=probT[t][:, s * w:(s + 1) * w],
                                    in_=slot,
                                    func=AF.Exp, bias=btb[h % 2][:, col:col + 1],
                                    scale=SCALE))
                            n_av += 1
                            buf_last_use[t] = n_av
                        uu += 1
            scalar.wait_ge(t_av.h, PLAN.avB[NH - 1])
            s_x5.inc(scalar.dma_start(
                out=xbuf[:].rearrange("p (st d) -> p st d", d=D),
                in_=xpg.ap()[0:512, :].rearrange("(st p) d -> p st d", p=P)), 16)
            ln_scal(scalar, f2A)
            scalar.wait_ge(fS["Bm"].h, f2A.a("Bm", 64))
            s_a5.inc(scalar.dma_start(out=xlt[:], in_=at_d.ap()), 16)
            scalar.wait_ge(s_out.h, 16)
            s_x5.inc(scalar.dma_start(
                out=xbuf[:].rearrange("p (st d) -> p st d", d=D),
                in_=xpg.ap()[512:1024, :].rearrange("(st p) d -> p st d", p=P)), 16)
            ln_scal(scalar, f2B)

        @block.tensor
        def _(tensor):
            tensor.wait_ge(s_cast.h, 32)
            tensor.wait_ge(s_ld.h, 16)
            for dt in range(16):
                ins = tensor.matmul(out=ps_tr[:, dt * 32:(dt + 1) * 32],
                                    lhsT=enc0_bf[:, dt * 128:(dt + 1) * 128],
                                    rhs=ident[0:32, 0:32], start=True, stop=True)
            s_encT.inc(ins)
            for dt in range(16, 32):
                ins = tensor.matmul(out=ps_x[:, (dt - 16) * 32:(dt - 15) * 32],
                                    lhsT=enc0_bf[:, dt * 128:(dt + 1) * 128],
                                    rhs=ident[0:32, 0:32], start=True, stop=True)
            s_encT.inc(ins)
            tensor.wait_ge(s_encE.h, 2)
            for dt in range(32):
                ins = tensor.matmul(out=ps_tr[:, dt * 8:(dt + 1) * 8],
                                    lhsT=enc1_bf[:, dt * 128:(dt + 1) * 128],
                                    rhs=ident[0:8, 0:8], start=True, stop=True)
            s_encT.inc(ins)
            tensor.wait_ge(s_encE.h, 3)
            tensor.wait_ge(s_wsum.h, 1)
            tensor.wait_ge(s_cast.h, 64)
            tensor.wait_ge(s_ld.h, 48)
            for ot in range(32):
                ins = tensor.matmul(out=ps_x[0:32, 0:512],
                                    lhsT=w0s[:, ot * 32:(ot + 1) * 32],
                                    rhs=m_bf[:, ot * 512:(ot + 1) * 512],
                                    start=(ot == 0), stop=(ot == 31))
            s_gmm.inc(ins)
            for ot in range(32):
                ins = tensor.matmul(out=ps_tr[0:8, 0:512],
                                    lhsT=w1s[:, ot * 8:(ot + 1) * 8],
                                    rhs=m_bf[:, ot * 512:(ot + 1) * 512],
                                    start=(ot == 0), stop=(ot == 31))
            s_gmm.inc(ins)
            tensor.wait_ge(s_gev.h, 2)
            ffn_tens(tensor, fA, prep=True, gl_wait=True)
            ffn_tens(tensor, fB, prep=False, gl_wait=False,
                     pre=((s_wpe.h, 256),))
            qkv_tens(tensor, qB, 1, pre=((fS["Ev"].h, fB.a("Ev", 2)),))
            qkv_tens(tensor, qA, 0, pre=((qS["Ke"].h, qB.a("Ke", 5)),
                                         (qS["Qe"].h, qB.a("Qe", 32))))
            qA.w(tensor, "Qe", 32)
            qA.w(tensor, "Ke", 5)
            fB.w(tensor, "Be", 64)
            tensor.wait_ge(s_kv.h, 16 * 36)
            tensor.wait_ge(s_kvg.h, 16 * 12)
            n_sc = 0
            n_ex = 0
            n_av = 0
            for h in range(NH):
                w = WH[h]
                ns = 512 // w
                tensor.wait_ge(s_q[h % 2].h, 16 * (h // 2 + 1))
                if h > 0:
                    tensor.wait_ge(v_at.h, 2 * h)
                LAG = 2
                pend = []

                def emit_av(tensor, i, t, n_ex_t, stop):
                    tensor.wait_ge(a_ex.h, n_ex_t)
                    for j in range(4):
                        st_flag = _is_diag_slot(i) and (t == 0)
                        ins = tensor.matmul(
                            out=ps4[:, j * 512:j * 512 + 129],
                            lhsT=probT[t][:, j * 128:(j + 1) * 128],
                            rhs=viti[:, i * 516 + t * 129:i * 516 + (t + 1) * 129],
                            start=st_flag, stop=stop,
                            skip_group_check=True)
                    t_av.inc(ins)

                for cb in range(2):
                    nu = UA[h] if cb == 0 else UB[h]
                    if cb == 1:
                        for (pi, pt, pn, pl) in pend:
                            emit_av(tensor, pi, pt, pn, pl)
                        pend = []
                        tensor.wait_ge(v_an.h, 2 * h + 1)
                        if h > 0:
                            tensor.wait_ge(v_at.h, 2 * h)
                        for j in range(4):
                            ins = tensor.matmul(
                                out=ps_tr[:, j * 128:(j + 1) * 128],
                                lhsT=anat[:, j * 128:(j + 1) * 128],
                                rhs=ident[:], start=True, stop=True)
                        t_tr.inc(ins)
                    for u in range(nu):
                        i = u if cb == 0 else 8 + u
                        for t in range(4):
                            for s in range(ns):
                                if n_sc >= 3:
                                    tensor.wait_ge(a_ex.h, n_sc - 2)
                                t_sc.inc(tensor.matmul(
                                    out=sc_slot(n_sc, w),
                                    lhsT=ktit[:, i * 512 + t * 128:i * 512 + (t + 1) * 128],
                                    rhs=qbuf[h % 2][:, cb * 512 + s * w:cb * 512 + (s + 1) * w],
                                    start=True, stop=True))
                                n_sc += 1
                            n_ex += ns
                            pend.append((i, t, n_ex,
                                         cb == 1 and u == nu - 1 and t == 3))
                            n_av += 1
                            if len(pend) > LAG:
                                pi, pt, pn, pl = pend.pop(0)
                                emit_av(tensor, pi, pt, pn, pl)
                for (pi, pt, pn, pl) in pend:
                    emit_av(tensor, pi, pt, pn, pl)
                pend = []
                idx = h * 2 + 1
                tensor.wait_ge(v_an.h, idx + 1)
                tensor.wait_ge(v_at.h, idx)
                for j in range(4):
                    ins = tensor.matmul(out=ps_tr[:, j * 128:(j + 1) * 128],
                                        lhsT=anat[:, j * 128:(j + 1) * 128],
                                        rhs=ident[:], start=True, stop=True)
                t_tr.inc(ins)
            for cb in range(2):
                if cb == 0:
                    tensor.wait_ge(v_at.h, 64)
                else:
                    f2A.w(tensor, "Be", 64)
                    tensor.wait_ge(s_a5.h, 16)
                for es in range(16):
                    gi = cb * 16 + es
                    tensor.wait_ge(s_w5[gi % 2].h, 16 * (gi // 2 + 1))
                    wts = wt0 if gi % 2 == 0 else wt1
                    for st in range(4):
                        ti = gi * 4 + st
                        if cb == 0 and ti >= 4:
                            tensor.wait_ge(v_op.h, ti - 3)
                        elif cb == 1 and ti >= 68:
                            tensor.wait_ge(v_op.h, ti - 2)
                        pa = slot4(ti)
                        for ht in range(32):
                            ins = tensor.matmul(
                                out=pa,
                                lhsT=xlt[:, ht * 512 + st * 128:ht * 512 + (st + 1) * 128],
                                rhs=wts[:, ht * 256:(ht + 1) * 256],
                                start=(ht == 0), stop=(ht == 31))
                        t_op.inc(ins)
                if cb == 0:
                    ffn_tens(tensor, f2A, prep=False, gl_wait=False,
                             pre=((v_op.h, 65),))
                else:
                    ffn_tens(tensor, f2B, prep=False, gl_wait=False,
                             pre=((v_op.h, 130),))

    return nc


_CACHE = {}


def kernel(**inputs):
    x = np.asarray(inputs["L0_attn_hidden_states"])[0]
    wqkv = np.ascontiguousarray(inputs["L0_attn_w_qkv"], np.float32)
    wcv = np.ascontiguousarray(inputs["L0_attn_w_c"], np.float32)
    wbase = np.ascontiguousarray(inputs["L0_ffn_W_base"], np.float32)
    bbase = np.ascontiguousarray(inputs["L0_ffn_b_base"], np.float32).reshape(1, D)
    enc0 = np.ascontiguousarray(inputs["L0_ffn_enc0"], np.float32)
    enc1 = np.ascontiguousarray(inputs["L0_ffn_enc1"], np.float32)
    coefs0 = np.ascontiguousarray(inputs["L0_ffn_coefs0"], np.float32)
    coefs1 = np.ascontiguousarray(inputs["L0_ffn_coefs1"], np.float32)
    mask0 = np.ascontiguousarray(inputs["L0_ffn_mask0"]).view(np.uint8)
    mask1 = np.ascontiguousarray(inputs["L0_ffn_mask1"]).view(np.uint8)

    if "nc" not in _CACHE:
        _CACHE["nc"] = build_nc()
        _CACHE["M"] = _rfom_matrix()
        _CACHE["m0"] = _m0_mask()
        _CACHE["ident"] = np.eye(P).astype(BFNP)
        _CACHE["ones"] = np.ones((1, P), BFNP)
        _CACHE["tables"] = [_host_tables(c) for c in range(NCORE)]
    nc = _CACHE["nc"]
    M = _CACHE["M"]

    in_maps = []
    for c in range(NCORE):
        off, bias = _CACHE["tables"][c]
        xs = np.concatenate([x[512 * c:512 * (c + 1)],
                             x[512 * (15 - c):512 * (16 - c)]], axis=0)
        in_maps.append(dict(
            xin=np.ascontiguousarray(xs, np.float32),
            wqkv=wqkv, wc=wcv, wbase=wbase, bbase=bbase,
            enc0=enc0, enc1=enc1, coefs0=coefs0, coefs1=coefs1,
            cmask0=mask0, cmask1=mask1,
            msl=np.ascontiguousarray(M[:, 512 * c:512 * (c + 1)]),
            biastbl=bias, m0in=_CACHE["m0"],
            identin=_CACHE["ident"], onesin=_CACHE["ones"], offin=off,
        ))
    res = run_bass_kernel_spmd(nc, in_maps, core_ids=list(range(NCORE)))
    out = np.empty((S, D), np.float32)
    for c in range(NCORE):
        xo = np.asarray(res.results[c]["xout"])
        out[512 * c:512 * (c + 1)] = xo[0:512]
        out[512 * (15 - c):512 * (16 - c)] = xo[512:1024]
    return out.reshape(1, S, D)


if __name__ == "__main__":
    build_nc()
    print("build ok")


# version 16 (final)
# speedup vs baseline: 1.0078x; 1.0078x over previous
import math
import numpy as np
import ml_dtypes
import concourse.bass as bass
import concourse.mybir as mybir
from concourse.bass_utils import run_bass_kernel_spmd

F32 = mybir.dt.float32
BF16 = mybir.dt.bfloat16
U8 = mybir.dt.uint8
I32 = mybir.dt.int32
AF = mybir.ActivationFunctionType
ALU = mybir.AluOpType
BFNP = ml_dtypes.bfloat16

P = 128
S = 8192
D = 4096
NH = 32
DH = 128
SHARD = 1024
NCORE = 8
NIT = 17
NSLOT = 24
SCALE = 1.0 / math.sqrt(DH)
ROFF = (0, 129, 258, 512)
NEG = -1.0e30


def _alibi_slopes(n=NH):
    cp2 = 2 ** int(math.floor(math.log2(n)))
    base = 2.0 ** (-(2.0 ** (-(math.log2(cp2) - 3))))
    slopes = base ** np.arange(1, cp2 + 1, dtype=np.float64)
    if cp2 != n:
        extra_base = 2.0 ** (-(2.0 ** (-(math.log2(2 * cp2) - 3))))
        rem = min(cp2, n - cp2)
        slopes = np.concatenate(
            [slopes, extra_base ** np.arange(1, 1 + 2 * rem, 2, dtype=np.float64)])
    return slopes


SLOPES = _alibi_slopes()


def _head_width(h):
    w = 512
    while SLOPES[h] * w > 64.0:
        w //= 2
    return max(w, 64)


WH = [_head_width(h) for h in range(NH)]

T_SKIP = 200.0


def _n_units(h):
    n = 1
    while n < 17 and SLOPES[h] * (512 * (n - 1) + WH[h]) < T_SKIP:
        n += 1
    return n


NUNIT = [_n_units(h) for h in range(NH)]
UA = [min(NUNIT[h], 8) for h in range(NH)]
UB = [min(NUNIT[h], 16) for h in range(NH)]

_BIAS_COLS = {}
_BIAS_H0 = {}
_nb = 0
for _h in range(NH):
    _BIAS_H0[_h] = _nb
    for _uu in range(UA[_h] + UB[_h]):
        for _t in range(4):
            for _s in range(512 // WH[_h]):
                _BIAS_COLS[(_h, _uu, _t, _s)] = _nb
                _nb += 1
NBIAS = _nb
NBH = {h: (UA[h] + UB[h]) * 4 * (512 // WH[h]) for h in range(NH)}


def _slots_for_core(c):
    a = [(0, c - u, True) if u <= c else (0, 0, False) for u in range(8)]
    b = [(1, 15 - c - u, True) if u <= 15 - c else (1, 0, False)
         for u in range(16)]
    return a + b


def _rfom_matrix():
    n = D
    sqrt_n = math.sqrt(n)
    pair_scale = math.sqrt(n / 2.0)
    max_f = (n - 1) // 2
    c = np.eye(n, dtype=np.float64)
    dc = (c[:, 0] * sqrt_n)[:, None].astype(np.complex128)
    cos = c[:, 1:1 + 2 * max_f:2] * pair_scale
    sin = c[:, 2:2 + 2 * max_f:2] * pair_scale
    mid = (cos - 1j * sin).astype(np.complex128)
    parts = [dc, mid]
    if n % 2 == 0:
        parts.append((c[:, -1] * sqrt_n)[:, None].astype(np.complex128))
    spectrum = np.concatenate(parts, axis=1)
    return np.fft.irfft(spectrum, n=n, axis=1).astype(np.float32)


def _host_tables(c):
    slots = _slots_for_core(c)
    off = np.zeros((1, 80), np.int32)
    for i, (cb, sks, real) in enumerate(slots):
        rank, half = (sks, 0) if sks < NCORE else (15 - sks, 1)
        off[0, i] = (rank * 128) * 1024 + half * 512
        off[0, 24 + i] = (rank * 1024 + half * 512) * 129
    bias = np.full((P, NBIAS), NEG, np.float32)
    p = np.arange(P, dtype=np.float64)
    for h in range(NH):
        w = WH[h]
        sl = SLOPES[h]
        for uu in range(UA[h] + UB[h]):
            i = uu if uu < UA[h] else 8 + (uu - UA[h])
            cb, sks, real = slots[i]
            if not real:
                continue
            ch_sup = c if cb == 0 else 15 - c
            for t in range(4):
                sk = sks * 512 + 128 * t + p
                for s in range(512 // w):
                    sq_hi = 512 * ch_sup + s * w + w
                    bias[:, _BIAS_COLS[(h, uu, t, s)]] = (sl * (sk - sq_hi)).astype(np.float32)
    return off, bias


def _m0_mask():
    m = np.zeros((P, 1024), np.float32)
    jj = np.arange(1024)[None, :]
    pp = np.arange(P)[:, None]
    m[jj < pp + 512] = NEG
    return m


def _mask_delta(t, s, w):
    return 128 * t - w * s


def _is_diag_slot(i):
    return i in (0, 8)


class _AttnPlan:
    def __init__(self):
        self.n_sc_after_h = []
        self.avA = []
        self.avB = []
        n_sc = 0
        n_av = 0
        for h in range(NH):
            ns = 512 // WH[h]
            n_sc += (UA[h] + UB[h]) * 4 * ns
            self.n_sc_after_h.append(n_sc)
            n_av += UA[h] * 4
            self.avA.append(n_av)
            n_av += UB[h] * 4
            self.avB.append(n_av)


PLAN = _AttnPlan()


class _Sem:
    def __init__(self, ctx, nc, name):
        self.h = ctx.enter_context(nc.semaphore(name))

    def inc(self, instr, k=1):
        instr.then_inc(self.h, k)
        return instr


FTOT = dict(Ln=4, Rs=8, Ap=4, Tr=32, Xl=32, E=2, Ev=2, Wt0=128, Wt1=128,
            Bm=64, Be=64, Fin=1)
QTOT = dict(Ln=4, Rs=8, Ap=4, Tr=32, Xl=32, Kw=16, Km=5, Ke=5, Qw0=128, Qw1=128,
            Qm=32, Qe=32, Qs0=256, Qs1=256)


class _CS:

    def __init__(self, sems, tot, idx):
        self.s, self.tot, self.i = sems, tot, idx

    def a(self, k, n):
        return self.tot[k] * self.i + n

    def w(self, eng, k, n):
        eng.wait_ge(self.s[k].h, self.a(k, n))

    def inc(self, k, ins, d=1):
        ins.then_inc(self.s[k].h, d)
        return ins


def build_nc():
    from contextlib import ExitStack
    nc = bass.Bass("TRN2", target_bir_lowering=False)
    nc.has_collectives = True

    def DI(name, shape, dt):
        return nc.dram_tensor(name, shape, dt, kind="ExternalInput")

    xin = DI("xin", [SHARD, D], F32)
    wqkv = DI("wqkv", [D, D + 2 * DH], F32)
    wc = DI("wc", [D, D], F32)
    wbase = DI("wbase", [D, D], F32)
    bbase = DI("bbase", [1, D], F32)
    enc0 = DI("enc0", [32, D], F32)
    enc1 = DI("enc1", [8, D], F32)
    coefs0 = DI("coefs0", [3, D, 32], F32)
    coefs1 = DI("coefs1", [3, D, 8], F32)
    cmask0 = DI("cmask0", [3, D, 32], U8)
    cmask1 = DI("cmask1", [3, D, 8], U8)
    msl = DI("msl", [D, 512], F32)
    biastbl = DI("biastbl", [P, NBIAS], F32)
    m0in = DI("m0in", [P, 1024], F32)
    identin = DI("identin", [P, P], BF16)
    onesin = DI("onesin", [1, P], BF16)
    offin = DI("offin", [1, 80], I32)

    xout = nc.dram_tensor("xout", [SHARD, D], F32, kind="ExternalOutput")

    WT_d = nc.dram_tensor("WT_d", [16 * P, 32 * 256], BF16)
    wcb_d = nc.dram_tensor("wcb_d", [16 * P, 32 * 256], BF16)
    wqb_d = nc.dram_tensor("wqb_d", [D, D + 2 * DH], BF16)
    qt_d = nc.dram_tensor("qt_d", [NH * P, 1024], BF16)
    at_d = nc.dram_tensor("at_d", [P, NH * 512], BF16)
    xpg = nc.dram_tensor("xpg", [SHARD, D], F32)
    g_kin = nc.dram_tensor("g_kin", [P, 1024], BF16)
    g_kout = nc.dram_tensor("g_kout", [NCORE * P, 1024], BF16)
    g_vin = nc.dram_tensor("g_vin", [1024, 129], BF16)
    g_vout = nc.dram_tensor("g_vout", [S, 129], BF16)
    g_gin = nc.dram_tensor("g_gin", [40, 512], BF16)
    g_gout = nc.dram_tensor("g_gout", [NCORE * 40, 512], BF16)

    ctx = ExitStack()
    with ctx:
        def at(name, shape, dt, off):
            return nc.alloc_sbuf_tensor_at(name, shape, dt, offset=off)

        BASE = 16512
        XO, WO, SO = BASE, BASE + 65536, BASE + 131072
        xbuf = at("xbuf", [P, 4 * D], F32, XO)
        ktit = at("ktit", [P, NSLOT * 512], BF16, XO)
        viti = at("viti", [P, NSLOT * 516], BF16, XO + 24576)
        m0m = at("m0m", [P, 1024], F32, XO + 49408)
        btb = [at(f"btb{k}", [P, 768], F32, XO + 53504 + k * 3072)
               for k in range(2)]
        qw2 = at("qw2", [P, 32 * 256], BF16, XO + 32768)
        assert 53504 + 6144 <= 65536

        xlt = at("xlt", [P, 32 * 512], BF16, WO)
        wt0 = at("wt0", [P, 32 * 256], BF16, WO + 32768)
        wt1 = at("wt1", [P, 32 * 256], BF16, WO + 49152)
        coefs_f = at("coefs_f", [P, 3840], F32, WO)
        cmask_f = at("cmask_f", [P, 3840], F32, WO + 15360)
        m_bf = at("m_bf", [P, 32 * 512], BF16, WO + 32768)

        so = [SO]

        def sb(name, shape, dt):
            t = at(name, shape, dt, so[0])
            nbytes = shape[1] * {F32: 4, BF16: 2, U8: 1, I32: 4}[dt]
            so[0] += (nbytes + 63) // 64 * 64
            return t

        g0p = sb("g0p", [32, D], BF16)
        g1p = sb("g1p", [8, D], BF16)
        b_bf = sb("b_bf", [1, D], BF16)
        _enc_off = so[0]
        enc0_bf = sb("enc0_bf", [32, D], BF16)
        enc1_bf = sb("enc1_bf", [8, D], BF16)
        wnat2 = at("wnat2", [P, D], BF16, _enc_off)
        wnat = at("wnat", [P, D], BF16, _enc_off + 8192)
        qb4 = [wnat2[:, 0:2048], wnat2[:, 2048:4096],
               wnat[:, 0:2048], wnat[:, 2048:4096]]
        enc0T = sb("enc0T", [P, 32 * 32], BF16)
        enc1T = sb("enc1T", [P, 32 * 8], BF16)
        e0bf = sb("e0bf", [32, 512], BF16)
        e1bf = sb("e1bf", [8, 512], BF16)
        ktown = sb("ktown", [P, 1024], BF16)
        vown = sb("vown", [P, 8 * 129], BF16)
        ident = sb("ident", [P, P], BF16)
        ones1 = sb("ones1", [1, P], BF16)
        w0s = sb("w0s", [P, 32 * 32], BF16)
        w1s = sb("w1s", [P, 32 * 8], BF16)
        xnst = sb("xnst", [P, D], BF16)
        stats = sb("stats", [P, 48], F32)
        mv = sb("mv", [P, 8], F32)
        lnv = sb("lnv", [P, 4], F32)
        rstd = sb("rstd", [P, 4], F32)
        epst = sb("epst", [P, 1], F32)
        recip = sb("recip", [P, 8], F32)
        offt = sb("offt", [1, 80], I32)
        _pq_off = so[0]
        probT = [sb(f"pt{k}", [P, 512], BF16) for k in range(4)]
        qbuf = [sb(f"qb{k}", [P, 1024], BF16) for k in range(2)]
        anat = sb("anat", [P, 512], BF16)
        atst = sb("atst", [P, 512], BF16)
        gst = sb("gst", [40, 512], BF16)
        kvstg = sb("kvstg", [P, 512], BF16)
        cmask_u = sb("cmask_u", [P, 3840], U8)
        wstgAB = at("wstgAB", [P, 2 * D], BF16, _pq_off)
        so[0] = max(so[0], _pq_off + 16384)
        kvb = [sb("kvb0", [P, 256], BF16), sb("kvb1", [P, 256], BF16)]
        assert so[0] <= 229344, f"S overflow {so[0]}"

        ps4 = nc.alloc_psum_tensor("ps4", [P, 2048], F32)
        ps_sc = nc.alloc_psum_tensor("ps_sc", [P, 1024], F32)
        ps_tr = nc.alloc_psum_tensor("ps_tr", [P, 512], F32)
        ps_x = nc.alloc_psum_tensor("ps_x", [P, 512], F32)

        def slot4(idx, width=256):
            return ps4[:, (idx % 4) * 512:(idx % 4) * 512 + width]

        def sc_slot(k, width):
            r = k % 3
            if r == 0:
                return ps_sc[:, 0:width]
            if r == 1:
                return ps_sc[:, 512:512 + width]
            return ps_x[:, 0:width]

        def mkS(name):
            return _Sem(ctx, nc, name)

        s_ld = mkS("s_ld")
        s_cast = mkS("s_cast")
        s_wsum = mkS("s_wsum")
        s_encT = mkS("s_encT")
        s_encE = mkS("s_encE")
        s_gmm = mkS("s_gmm")
        s_gev = mkS("s_gev")
        s_gst = mkS("s_gst")
        s_cc = mkS("s_cc")
        s_gld = mkS("s_gld")
        s_wpt = mkS("s_wpt")
        s_wpe = mkS("s_wpe")
        s_wps = [mkS("s_wps0"), mkS("s_wps1")]
        s_xpg = mkS("s_xpg")
        s_xld = mkS("s_xld")
        s_kvst = mkS("s_kvst")
        s_wcs = [mkS("s_wcs0"), mkS("s_wcs1")]
        s_wcc = [mkS("s_wcc0"), mkS("s_wcc1")]
        s_qpc = mkS("s_qpc")
        s_qpk = mkS("s_qpk")
        s_qps = [mkS("s_qps0"), mkS("s_qps1")]
        s_tbl = mkS("s_tbl")
        s_kv = mkS("s_kv")
        s_kvg = mkS("s_kvg")
        s_bt = [mkS("s_bt0"), mkS("s_bt1")]
        s_wc = [mkS("s_wc0"), mkS("s_wc1")]
        s_q = [mkS("s_q0"), mkS("s_q1")]
        t_sc = mkS("t_sc")
        a_ex = mkS("a_ex")
        v_mk = mkS("v_mk")
        t_av = mkS("t_av")
        v_rc = mkS("v_rc")
        v_an = mkS("v_an")
        t_tr = mkS("t_tr")
        v_at = mkS("v_at")
        s_at = mkS("s_at")
        s_x5 = mkS("s_x5")
        s_a5 = mkS("s_a5")
        s_w5 = [mkS("s_w50"), mkS("s_w51")]
        t_op = mkS("t_op")
        v_op = mkS("v_op")
        s_out = mkS("s_out")
        vch = mkS("vch")
        VCH = [0]

        def vedge(vector, ins):
            VCH[0] += 1
            ins.then_inc(vch.h, 1)
            vector.wait_ge(vch.h, VCH[0])

        fS = {k: mkS(f"f{k}") for k in FTOT}
        qS = {k: mkS(f"q{k}") for k in QTOT}
        fA, fB, f2A, f2B = (_CS(fS, FTOT, i) for i in range(4))
        qB, qA = (_CS(qS, QTOT, i) for i in range(2))

        block = ctx.enter_context(nc.Block())

        def ln_vec(vector, q, waits, st_waits=None):
            for hh, val in waits:
                vector.wait_ge(hh, val)
            for st in range(4):
                if st > 0:
                    q.w(vector, "Ln", st)
                for hh, val in (st_waits or {}).get(st, ()):
                    vector.wait_ge(hh, val)
                for k in range(8):
                    ins = vector.bn_stats(
                        out=stats[:, 6 * k:6 * k + 6],
                        in_=xbuf[:, st * D + 512 * k:st * D + 512 * (k + 1)])
                vedge(vector, ins)
                q.inc("Ln", vector.bn_aggr(out=mv[:, 2 * st:2 * st + 2], in_=stats[:]))
            for st in range(4):
                q.w(vector, "Rs", 2 * (st + 1))
                if st > 0:
                    q.w(vector, "Tr", 8 * st)
                q.inc("Ap", vector.tensor_scalar(
                    out=xnst[:], in0=xbuf[:, st * D:(st + 1) * D],
                    scalar1=mv[:, 2 * st:2 * st + 1], scalar2=rstd[:, st:st + 1],
                    op0=ALU.subtract, op1=ALU.mult))
                for g in range(8):
                    gi = st * 8 + g
                    q.w(vector, "Tr", gi + 1)
                    dst = xlt[:].rearrange("p (dt s) -> p dt s", s=512)[
                        :, g * 4:(g + 1) * 4, st * 128:(st + 1) * 128]
                    src = (ps_tr if g % 2 == 0 else ps_x)[:].rearrange(
                        "p (k c) -> p k c", c=128)
                    q.inc("Xl", vector.tensor_copy(out=dst, in_=src))

        def ln_scal(scalar, q):
            for st in range(4):
                q.w(scalar, "Ln", st + 1)
                q.inc("Rs", scalar.activation(out=lnv[:, st:st + 1],
                                              in_=mv[:, 2 * st + 1:2 * st + 2],
                                              func=AF.Ln, bias=epst[:, 0:1], scale=1.0))
                q.w(scalar, "Rs", 2 * st + 1)
                q.inc("Rs", scalar.activation(out=rstd[:, st:st + 1],
                                              in_=lnv[:, st:st + 1],
                                              func=AF.Exp, bias=0.0, scale=-0.5))

        def ln_tens(tensor, q):
            for st in range(4):
                q.w(tensor, "Ap", st + 1)
                for g in range(8):
                    gi = st * 8 + g
                    pslot = ps_tr if g % 2 == 0 else ps_x
                    if gi >= 2:
                        q.w(tensor, "Xl", gi - 1)
                    for k in range(4):
                        dsl = g * 4 + k
                        ins = tensor.matmul(out=pslot[:, k * 128:(k + 1) * 128],
                                            lhsT=xnst[:, dsl * 128:(dsl + 1) * 128],
                                            rhs=ident[:], start=True, stop=True)
                    q.inc("Tr", ins)

        def ffn_tens(tensor, q, prep, gl_wait, pre=()):
            for hh, val in pre:
                tensor.wait_ge(hh, val)
            ln_tens(tensor, q)
            q.w(tensor, "Xl", 32)
            for dt in range(32):
                ins = tensor.matmul(out=ps_x[0:32, 0:512],
                                    lhsT=enc0T[:, dt * 32:(dt + 1) * 32],
                                    rhs=xlt[:, dt * 512:(dt + 1) * 512],
                                    start=(dt == 0), stop=(dt == 31))
            q.inc("E", ins)
            for dt in range(32):
                ins = tensor.matmul(out=ps_tr[0:8, 0:512],
                                    lhsT=enc1T[:, dt * 8:(dt + 1) * 8],
                                    rhs=xlt[:, dt * 512:(dt + 1) * 512],
                                    start=(dt == 0), stop=(dt == 31))
            q.inc("E", ins)
            q.w(tensor, "Ev", 2)
            if gl_wait:
                tensor.wait_ge(s_gld.h, 32)
            if prep:
                for ot in range(6):
                    wprep_tens(tensor, ot)
            for os_ in range(16):
                if prep:
                    for ot in (2 * os_ + 6, 2 * os_ + 7):
                        if ot < 32:
                            wprep_tens(tensor, ot)
                q.w(tensor, f"Wt{os_ % 2}", 16 * (os_ // 2 + 1))
                wts = wt0 if os_ % 2 == 0 else wt1
                for st in range(4):
                    idx = os_ * 4 + st
                    if idx >= 4:
                        q.w(tensor, "Be", idx - 3)
                    pa = slot4(idx)
                    tensor.matmul(out=pa, lhsT=ones1[:],
                                  rhs=b_bf[0:1, os_ * 256:(os_ + 1) * 256],
                                  start=True, stop=False)
                    for dt in range(32):
                        tensor.matmul(out=pa,
                                      lhsT=xlt[:, dt * 512 + st * 128:dt * 512 + (st + 1) * 128],
                                      rhs=wts[:, dt * 256:(dt + 1) * 256],
                                      start=False, stop=False)
                    tensor.matmul(out=pa, lhsT=e0bf[:, st * 128:(st + 1) * 128],
                                  rhs=g0p[:, os_ * 256:(os_ + 1) * 256],
                                  start=False, stop=False)
                    q.inc("Bm", tensor.matmul(
                        out=pa, lhsT=e1bf[:, st * 128:(st + 1) * 128],
                        rhs=g1p[:, os_ * 256:(os_ + 1) * 256],
                        start=False, stop=True))

        def ffn_vec(vector, q, waits, prep):
            ln_vec(vector, q, waits)
            q.w(vector, "E", 1)
            q.inc("Ev", vector.tensor_copy(out=e0bf[:], in_=ps_x[0:32, 0:512]))
            q.w(vector, "E", 2)
            q.inc("Ev", vector.tensor_copy(out=e1bf[:], in_=ps_tr[0:8, 0:512]))
            if prep:
                for ot in range(6):
                    wprep_vec(vector, ot)
            for os_ in range(16):
                if prep:
                    for ot in (2 * os_ + 6, 2 * os_ + 7):
                        if ot < 32:
                            wprep_vec(vector, ot)
                for st in range(4):
                    idx = os_ * 4 + st
                    q.w(vector, "Bm", idx + 1)
                    q.inc("Be", vector.scalar_tensor_tensor(
                        out=xbuf[:, st * D + os_ * 256:st * D + (os_ + 1) * 256],
                        in0=slot4(idx), scalar=0.5,
                        in1=xbuf[:, st * D + os_ * 256:st * D + (os_ + 1) * 256],
                        op0=ALU.mult, op1=ALU.add))
            q.inc("Fin", vector.tensor_copy(out=epst[:], in_=epst[:]))

        def ffn_sync(sync, q, prep):
            if prep:
                sync.wait_ge(s_gmm.h, 2)
                wprep_sync(sync, 0)
            for os_ in range(16):
                if prep:
                    sync.wait_ge(s_wps[0].h, 16 * (os_ + 1))
                else:
                    sync.wait_ge(s_wps[0].h, 256)
                if os_ >= 2:
                    q.w(sync, "Bm", 4 * (os_ - 1))
                q.inc(f"Wt{os_ % 2}", sync.dma_start(
                    out=(wt0 if os_ % 2 == 0 else wt1)[:],
                    in_=WT_d.ap()[os_ * P:(os_ + 1) * P, :]), 16)
                if prep and os_ + 1 < 16:
                    wprep_sync(sync, os_ + 1)

        def wprep_tens(tensor, ot):
            tensor.wait_ge(s_wc[ot % 2].h, 16 * (ot // 2 + 1))
            src = wnat2 if ot % 2 == 0 else wnat
            for g in range(8):
                gi = ot * 8 + g
                pslot = ps_tr if g % 2 == 0 else ps_x
                if gi >= 2:
                    tensor.wait_ge(s_wpe.h, gi - 1)
                for k in range(4):
                    dsl = g * 4 + k
                    ins = tensor.matmul(out=pslot[:, k * 128:(k + 1) * 128],
                                        lhsT=src[:, dsl * 128:(dsl + 1) * 128],
                                        rhs=ident[:], start=True, stop=True)
                s_wpt.inc(ins)

        def wprep_vec(vector, ot):
            if ot == 0:
                vector.wait_ge(s_gst.h, 16)
            if ot >= 2:
                vector.wait_ge(s_wps[0].h, 16 * (ot // 2))
            half = ot % 2
            for g in range(8):
                gi = ot * 8 + g
                pslot = ps_tr if g % 2 == 0 else ps_x
                vector.wait_ge(s_wpt.h, gi + 1)
                dst = wstgAB[:].rearrange("p (ds pair) -> p ds pair", pair=256)[
                    :, g * 4:(g + 1) * 4, half * 128:half * 128 + 128]
                s_wpe.inc(vector.tensor_copy(
                    out=dst, in_=pslot[:].rearrange("p (k c) -> p k c", c=128)))

        def wprep_sync(sync, os_):
            sync.wait_ge(s_wpe.h, 16 * (os_ + 1))
            s_wps[0].inc(sync.dma_start(
                out=WT_d.ap()[os_ * P:(os_ + 1) * P, :], in_=wstgAB[:]), 16)

        def q_store(eng, j):
            dt = j % 32
            half = j // 32
            eng.wait_ge(s_qpc.h, 512 + 16 * (j + 1))
            s_qps[half].inc(eng.dma_start(
                out=wqb_d.ap()[dt * 128:(dt + 1) * 128,
                               half * 2048:half * 2048 + 2048],
                in_=qb4[j % 4]), 16)

        def qkv_tens(tensor, q, cb, pre=()):
            for hh, val in pre:
                tensor.wait_ge(hh, val)
            ln_tens(tensor, q)
            q.w(tensor, "Xl", 32)
            q.w(tensor, "Kw", 16)
            for dt in range(32):
                ins = tensor.matmul(out=ps_x[:, 0:512],
                                    lhsT=wstgAB[:, dt * 256:dt * 256 + 128],
                                    rhs=xlt[:, dt * 512:(dt + 1) * 512],
                                    start=(dt == 0), stop=(dt == 31))
            q.inc("Km", ins)
            for st in range(4):
                for dt in range(32):
                    ins = tensor.matmul(
                        out=ps_tr[:, st * 128:(st + 1) * 128],
                        lhsT=xlt[:, dt * 512 + st * 128:dt * 512 + (st + 1) * 128],
                        rhs=wstgAB[:, dt * 256 + 128:dt * 256 + 256],
                        start=(dt == 0), stop=(dt == 31))
                q.inc("Km", ins)
            for h in range(NH):
                hp = h // 2
                q.w(tensor, f"Qw{hp % 2}", 16 * (hp // 2 + 1))
                if h >= 2:
                    q.w(tensor, "Qe", h - 1)
                wsl = wt0 if hp % 2 == 0 else qw2
                for dt in range(32):
                    ins = tensor.matmul(
                        out=ps_sc[:, (h % 2) * 512:(h % 2) * 512 + 512],
                        lhsT=wsl[:, dt * 256 + (h % 2) * 128:dt * 256 + (h % 2) * 128 + 128],
                        rhs=xlt[:, dt * 512:(dt + 1) * 512],
                        start=(dt == 0), stop=(dt == 31))
                q.inc("Qm", ins)

        def qkv_vec(vector, q, cb, waits, st_waits=None):
            ln_vec(vector, q, waits, st_waits)
            q.w(vector, "Km", 1)
            q.inc("Ke", vector.tensor_copy(out=ktown[:, cb * 512:(cb + 1) * 512],
                                           in_=ps_x[:, 0:512]))
            q.w(vector, "Km", 5)
            for st in range(4):
                q.inc("Ke", vector.tensor_copy(
                    out=vown[:, (cb * 4 + st) * 129:(cb * 4 + st) * 129 + 128],
                    in_=ps_tr[:, st * 128:(st + 1) * 128]))
            for h in range(NH):
                q.w(vector, "Qm", h + 1)
                if h >= 2:
                    q.w(vector, f"Qs{h % 2}", 16 * (h // 2))
                elif q.i > 0:
                    vector.wait_ge(q.s[f"Qs{h % 2}"].h, q.tot[f"Qs{h % 2}"] * q.i)
                q.inc("Qe", vector.tensor_copy(
                    out=qbuf[h % 2][:, 0:512],
                    in_=ps_sc[:, (h % 2) * 512:(h % 2) * 512 + 512]))

        def qkv_sync(sync, q, cb, kw_pre, hp_pre, mid=None):
            sync.wait_ge(s_qpk.h, 512)
            sync.wait_ge(s_wps[0].h, 256)
            for hh, vv in kw_pre:
                sync.wait_ge(hh, vv)
            q.inc("Kw", sync.dma_start(
                out=wstgAB[:].rearrange("p (dt c) -> p dt c", c=256),
                in_=wqb_d.ap()[:, D:D + 256].rearrange(
                    "(dt p) c -> p dt c", p=P)), 16)
            sync.wait_ge(s_qps[0].h, 512)
            for hh, vv in hp_pre:
                sync.wait_ge(hh, vv)
            for hp in range(16):
                if hp == 8:
                    if mid is not None:
                        mid()
                    sync.wait_ge(s_qps[1].h, 512)
                if hp >= 2:
                    q.w(sync, "Qm", 2 * hp - 2)
                q.inc(f"Qw{hp % 2}", sync.dma_start(
                    out=(wt0 if hp % 2 == 0 else qw2)[:].rearrange(
                        "p (dt c) -> p dt c", c=256),
                    in_=wqb_d.ap()[:, hp * 256:(hp + 1) * 256].rearrange(
                        "(dt p) c -> p dt c", p=P)), 16)
                for h in (2 * hp - 4, 2 * hp - 3):
                    if h >= 0:
                        q.w(sync, "Qe", h + 1)
                        q.inc(f"Qs{h % 2}", sync.dma_start(
                            out=qt_d.ap()[h * P:(h + 1) * P,
                                          cb * 512:(cb + 1) * 512],
                            in_=qbuf[h % 2][:, 0:512]), 16)
            for h in (28, 29, 30, 31):
                q.w(sync, "Qe", h + 1)
                q.inc(f"Qs{h % 2}", sync.dma_start(
                    out=qt_d.ap()[h * P:(h + 1) * P, cb * 512:(cb + 1) * 512],
                    in_=qbuf[h % 2][:, 0:512]), 16)

        @block.sync
        def _(sync):
            loads = [
                (ident[:], identin.ap()),
                (ones1[:], onesin.ap()),
                (offt[:], offin.ap()),
                (coefs_f[:, 0:3072].rearrange("p (eo r) -> p eo r", r=32),
                 coefs0.ap().rearrange("e (ot p) r -> p (e ot) r", p=P)),
                (coefs_f[:, 3072:3840].rearrange("p (eo r) -> p eo r", r=8),
                 coefs1.ap().rearrange("e (ot p) r -> p (e ot) r", p=P)),
                (cmask_u[:, 0:3072].rearrange("p (eo r) -> p eo r", r=32),
                 cmask0.ap().rearrange("e (ot p) r -> p (e ot) r", p=P)),
                (cmask_u[:, 3072:3840].rearrange("p (eo r) -> p eo r", r=8),
                 cmask1.ap().rearrange("e (ot p) r -> p (e ot) r", p=P)),
                (xbuf[:].rearrange("p (st d) -> p st d", d=D),
                 xin.ap()[0:512, :].rearrange("(st p) d -> p st d", p=P)),
            ]
            for dst, src in loads:
                s_ld.inc(sync.dma_start(out=dst, in_=src), 16)
            sync.wait_ge(s_gev.h, 2)
            s_gst.inc(sync.dma_start(out=g_gin.ap(), in_=gst[:]), 16)
            sync.wait_ge(s_cc.h, 1)
            s_gld.inc(sync.dma_start(
                out=g0p[:].rearrange("i (r o) -> i r o", o=512),
                in_=g_gout.ap().rearrange("(r i) o -> i r o", i=40)[0:32, :, :]), 16)
            s_gld.inc(sync.dma_start(
                out=g1p[:].rearrange("i (r o) -> i r o", o=512),
                in_=g_gout.ap().rearrange("(r i) o -> i r o", i=40)[32:40, :, :]), 16)
            ffn_sync(sync, fA, prep=True)

            def mid_half1():
                for j in range(32, 64):
                    q_store(sync, j)

            fA.w(sync, "Be", 64)
            s_xpg.inc(sync.dma_start(
                out=xpg.ap()[0:512, :].rearrange("(st p) d -> p st d", p=P),
                in_=xbuf[:].rearrange("p (st d) -> p st d", d=D)), 16)
            sync.wait_ge(s_xpg.h, 16)
            s_xld.inc(sync.dma_start(
                out=xbuf[:].rearrange("p (st d) -> p st d", d=D),
                in_=xin.ap()[512:1024, :].rearrange("(st p) d -> p st d", p=P)), 16)
            ffn_sync(sync, fB, prep=False)
            qB.w(sync, "Ap", 4)
            qB.w(sync, "Xl", 32)
            s_xpg.inc(sync.dma_start(
                out=xpg.ap()[512:1024, :].rearrange("(st p) d -> p st d", p=P),
                in_=xbuf[:].rearrange("p (st d) -> p st d", d=D)), 16)
            sync.wait_ge(s_xpg.h, 32)
            s_xld.inc(sync.dma_start(
                out=xbuf[:].rearrange("p (st d) -> p st d", d=D)[:, 0:2, :],
                in_=xpg.ap()[0:256, :].rearrange("(st p) d -> p st d", p=P)), 16)
            s_xld.inc(sync.dma_start(
                out=xbuf[:, 3 * D:4 * D], in_=xpg.ap()[384:512, :]), 16)
            qkv_sync(sync, qB, 1, kw_pre=[],
                     hp_pre=[(fS["Bm"].h, fB.a("Bm", 64)), (s_xpg.h, 32)],
                     mid=mid_half1)
            qB.w(sync, "Qm", 32)
            s_xld.inc(sync.dma_start(
                out=xbuf[:, 2 * D:3 * D], in_=xpg.ap()[256:384, :]), 16)
            qkv_sync(sync, qA, 0,
                     kw_pre=[(qS["Km"].h, qB.a("Km", 5)),
                             (qS["Qs0"].h, qB.a("Qs0", 256)),
                             (qS["Qs1"].h, qB.a("Qs1", 256))],
                     hp_pre=[(qS["Qm"].h, qB.a("Qm", 32)),
                             (qS["Ap"].h, qA.a("Ap", 4))])
            qA.w(sync, "Ke", 5)
            qB.w(sync, "Ke", 5)
            s_kvst.inc(sync.dma_start(out=g_kin.ap(), in_=ktown[:]), 16)
            s_kvst.inc(sync.dma_start(
                out=g_vin.ap().rearrange("(st p) v -> p st v", p=P),
                in_=vown[:].rearrange("p (st v) -> p st v", v=129)), 16)
            qA.w(sync, "Ap", 4)
            s_tbl.inc(sync.dma_start(out=m0m[:], in_=m0in.ap()), 16)
            sync.wait_ge(s_cc.h, 2)
            qA.w(sync, "Qm", 32)
            with sync.register("r_ko") as r_ko:
                for i in range(12):
                    sync.reg_load(r_ko, offt[0:1, i:i + 1])
                    s_kv.inc(sync.dma_start(
                        out=ktit[:, i * 512:(i + 1) * 512],
                        in_=bass.AP(g_kout, r_ko, [[1024, P], [1, 512]])), 16)
                sync.wait_ge(s_cc.h, 3)
                for i in range(12, 18):
                    sync.reg_load(r_ko, offt[0:1, 24 + i:25 + i])
                    s_kv.inc(sync.dma_start(
                        out=viti[:, i * 516:(i + 1) * 516].rearrange(
                            "p (t v) -> p t v", v=129),
                        in_=bass.AP(g_vout, r_ko, [[129, P], [129 * P, 4], [1, 129]])), 16)
            qA.w(sync, "Qs0", 256)
            qA.w(sync, "Qs1", 256)
            qB.w(sync, "Qs0", 256)
            qB.w(sync, "Qs1", 256)
            for h in range(NH):
                if h >= 2:
                    sync.wait_ge(t_sc.h, PLAN.n_sc_after_h[h - 2] + 1)
                s_q[h % 2].inc(sync.dma_start(
                    out=qbuf[h % 2][:],
                    in_=qt_d.ap()[h * P:(h + 1) * P, :]), 16)
                s_bt[h % 2].inc(sync.dma_start(
                    out=btb[h % 2][:, 0:NBH[h]],
                    in_=biastbl.ap()[:, _BIAS_H0[h]:_BIAS_H0[h] + NBH[h]]), 16)
                if h >= 1:
                    sync.wait_ge(v_at.h, 2 * (h - 1) + 2)
                    s_at.inc(sync.dma_start(
                        out=at_d.ap()[:, (h - 1) * 512:h * 512],
                        in_=atst[:]), 16)
            sync.wait_ge(v_at.h, 64)
            s_at.inc(sync.dma_start(
                out=at_d.ap()[:, 31 * 512:32 * 512], in_=atst[:]), 16)
            sync.wait_ge(t_av.h, PLAN.avB[NH - 1])
            sync.wait_ge(s_wcs[0].h, 256)
            sync.wait_ge(s_wcs[1].h, 256)
            for es in range(16):
                if es >= 2:
                    sync.wait_ge(t_op.h, 4 * (es - 1))
                s_w5[es % 2].inc(sync.dma_start(
                    out=(wt0 if es % 2 == 0 else wt1)[:],
                    in_=wcb_d.ap()[es * P:(es + 1) * P, :]), 16)
            sync.wait_ge(t_op.h, 64)
            ffn_sync(sync, f2A, prep=False)
            f2A.w(sync, "Be", 64)
            s_out.inc(sync.dma_start(
                out=xout.ap()[0:512, :].rearrange("(st p) d -> p st d", p=P),
                in_=xbuf[:].rearrange("p (st d) -> p st d", d=D)), 16)
            for es in range(16):
                gi = 16 + es
                if gi >= 18:
                    sync.wait_ge(t_op.h, 4 * (gi - 1))
                s_w5[gi % 2].inc(sync.dma_start(
                    out=(wt0 if gi % 2 == 0 else wt1)[:],
                    in_=wcb_d.ap()[es * P:(es + 1) * P, :]), 16)
            sync.wait_ge(t_op.h, 128)
            ffn_sync(sync, f2B, prep=False)
            f2B.w(sync, "Be", 64)
            s_out.inc(sync.dma_start(
                out=xout.ap()[512:1024, :].rearrange("(st p) d -> p st d", p=P),
                in_=xbuf[:].rearrange("p (st d) -> p st d", d=D)), 16)
            sync.wait_ge(s_out.h, 32)

        @block.gpsimd
        def _(gpsimd):
            s_cast.inc(gpsimd.dma_start(out=enc0_bf[:], in_=enc0.ap()), 16)
            s_cast.inc(gpsimd.dma_start(out=enc1_bf[:], in_=enc1.ap()), 16)
            s_cast.inc(gpsimd.dma_start(out=b_bf[:], in_=bbase.ap()), 16)
            s_cast.inc(gpsimd.dma_start(
                out=m_bf[:].rearrange("p (dt o) -> p dt o", o=512),
                in_=msl.ap().rearrange("(dt p) o -> p dt o", p=P)), 16)
            gpsimd.wait_ge(s_gst.h, 16)
            s_cc.inc(gpsimd.collective_compute(
                "AllGather", ALU.bypass, replica_groups=[list(range(NCORE))],
                ins=[g_gin.ap()], outs=[g_gout.ap()]))
            gpsimd.wait_ge(s_encT.h, 3)
            for ot in range(32):
                if ot >= 2:
                    gpsimd.wait_ge(s_wpt.h, 8 * (ot - 1))
                s_wc[ot % 2].inc(gpsimd.dma_start(
                    out=(wnat2 if ot % 2 == 0 else wnat)[:],
                    in_=wbase.ap()[ot * 128:(ot + 1) * 128, :]), 16)
            for k in range(32):
                if k >= 2:
                    gpsimd.wait_ge(s_qpk.h, 16 * (k - 1))
                s_qpc.inc(gpsimd.dma_start(
                    out=kvb[k % 2][:],
                    in_=wqkv.ap()[k * 128:(k + 1) * 128, D:D + 256]), 16)
            gpsimd.wait_ge(s_wpt.h, 256)
            for j in range(64):
                dt = j % 32
                half = j // 32
                pj = j - 4
                if pj >= 0:
                    if pj < 32:
                        gpsimd.wait_ge(s_qps[0].h, 16 * (pj + 1))
                    else:
                        gpsimd.wait_ge(s_qps[1].h, 16 * (pj - 31))
                s_qpc.inc(gpsimd.dma_start(
                    out=qb4[j % 4],
                    in_=wqkv.ap()[dt * 128:(dt + 1) * 128,
                                  half * 2048:half * 2048 + 2048]), 16)
            gpsimd.wait_ge(s_qps[0].h, 512)
            gpsimd.wait_ge(s_qps[1].h, 512)
            for ot in range(32):
                if ot >= 2:
                    gpsimd.wait_ge(s_wcs[ot % 2].h, 16 * (ot // 2))
                s_wcc[ot % 2].inc(gpsimd.dma_start(
                    out=(wnat2 if ot % 2 == 0 else wnat)[:],
                    in_=wc.ap()[ot * 128:(ot + 1) * 128, :]), 16)
                gpsimd.wait_ge(s_wcc[ot % 2].h, 16 * (ot // 2 + 1))
                dst = wcb_d.ap().rearrange(
                    "(es p) (ht e) -> p es ht e", p=P, e=256)[:, :, ot, :]
                src = (wnat2 if ot % 2 == 0 else wnat)[:].rearrange(
                    "p (es e) -> p es e", e=256)
                s_wcs[ot % 2].inc(gpsimd.dma_start(out=dst, in_=src), 16)
            gpsimd.wait_ge(s_kvst.h, 32)
            s_cc.inc(gpsimd.collective_compute(
                "AllGather", ALU.bypass, replica_groups=[list(range(NCORE))],
                ins=[g_kin.ap()], outs=[g_kout.ap()]))
            s_cc.inc(gpsimd.collective_compute(
                "AllGather", ALU.bypass, replica_groups=[list(range(NCORE))],
                ins=[g_vin.ap()], outs=[g_vout.ap()]))
            gpsimd.wait_ge(s_cc.h, 3)
            gpsimd.wait_ge(qS["Ap"].h, qA.a("Ap", 4))
            gpsimd.wait_ge(qS["Qm"].h, qA.a("Qm", 32))
            with gpsimd.register("r_vg") as r_vg:
                for i in range(12):
                    gpsimd.reg_load(r_vg, offt[0:1, 24 + i:25 + i])
                    s_kvg.inc(gpsimd.dma_start(
                        out=viti[:, i * 516:(i + 1) * 516].rearrange(
                            "p (t v) -> p t v", v=129),
                        in_=bass.AP(g_vout, r_vg, [[129, P], [129 * P, 4], [1, 129]])), 16)

        @block.vector
        def _(vector):
            vector.memset(epst[:], 1e-5)
            for st in range(8):
                vector.memset(vown[:, st * 129 + 128:st * 129 + 129], 1.0)
            vector.wait_ge(s_encT.h, 1)
            s_encE.inc(vector.tensor_copy(out=enc0T[:, 0:512], in_=ps_tr[:]))
            vector.wait_ge(s_encT.h, 2)
            s_encE.inc(vector.tensor_copy(out=enc0T[:, 512:1024], in_=ps_x[:]))
            vector.wait_ge(s_encT.h, 3)
            s_encE.inc(vector.tensor_copy(out=enc1T[:], in_=ps_tr[:, 0:256]))
            vector.wait_ge(s_ld.h, 112)
            vedge(vector, vector.tensor_copy(out=cmask_f[:], in_=cmask_u[:]))
            vedge(vector, vector.tensor_tensor(out=cmask_f[:], in0=cmask_f[:],
                                               in1=coefs_f[:], op=ALU.mult))
            vedge(vector, vector.tensor_tensor(
                out=cmask_f[:, 0:1024], in0=cmask_f[:, 0:1024],
                in1=cmask_f[:, 1024:2048], op=ALU.add))
            vedge(vector, vector.tensor_tensor(
                out=cmask_f[:, 0:1024], in0=cmask_f[:, 0:1024],
                in1=cmask_f[:, 2048:3072], op=ALU.add))
            vedge(vector, vector.tensor_tensor(
                out=cmask_f[:, 3072:3328], in0=cmask_f[:, 3072:3328],
                in1=cmask_f[:, 3328:3584], op=ALU.add))
            vedge(vector, vector.tensor_tensor(
                out=cmask_f[:, 3072:3328], in0=cmask_f[:, 3072:3328],
                in1=cmask_f[:, 3584:3840], op=ALU.add))
            vedge(vector, vector.tensor_copy(out=w0s[:], in_=cmask_f[:, 0:1024]))
            s_wsum.inc(vector.tensor_copy(out=w1s[:], in_=cmask_f[:, 3072:3328]))
            vector.wait_ge(s_gmm.h, 1)
            s_gev.inc(vector.tensor_scalar(out=gst[0:32, :], in0=ps_x[0:32, 0:512],
                                           scalar1=0.5, scalar2=None, op0=ALU.mult))
            vector.wait_ge(s_gmm.h, 2)
            s_gev.inc(vector.tensor_scalar(out=gst[32:40, :], in0=ps_tr[0:8, 0:512],
                                           scalar1=0.5, scalar2=None, op0=ALU.mult))
            ffn_vec(vector, fA, waits=[(s_ld.h, 128)], prep=True)
            ffn_vec(vector, fB, waits=[(s_xld.h, 16)], prep=False)
            qkv_vec(vector, qB, 1, waits=[(fS["Fin"].h, fB.a("Fin", 1)),
                                           (s_wps[0].h, 256)])
            qkv_vec(vector, qA, 0, waits=[(s_xld.h, 48)],
                    st_waits={2: ((s_xld.h, 64),)})
            vector.wait_ge(s_tbl.h, 16)
            n_sc = 0
            n_mk = 0
            n_av = 0
            def attn_evac_front(vector, h, cb):
                idx = h * 2 + cb
                vector.wait_ge(t_av.h, PLAN.avA[h] if cb == 0 else PLAN.avB[h])
                if idx > 0:
                    vector.wait_ge(v_an.h, idx)
                for j in range(4):
                    v_rc.inc(vector.reciprocal(
                        out=recip[:, j:j + 1],
                        in_=ps4[:, j * 512 + 128:j * 512 + 129]))
                vector.wait_ge(v_rc.h, 4 * (idx + 1))
                if idx > 0:
                    vector.wait_ge(t_tr.h, idx)
                for j in range(4):
                    ins = vector.tensor_scalar(
                        out=anat[:, j * 128:(j + 1) * 128],
                        in0=ps4[:, j * 512:j * 512 + 128],
                        scalar1=recip[:, j:j + 1], scalar2=None, op0=ALU.mult)
                v_an.inc(ins)

            def attn_evac_back(vector, h, cb):
                idx = h * 2 + cb
                vector.wait_ge(t_tr.h, idx + 1)
                if cb == 0:
                    v_at.inc(vector.tensor_copy(
                        out=xlt[:, h * 512:(h + 1) * 512], in_=ps_tr[:]))
                else:
                    if h > 0:
                        vector.wait_ge(s_at.h, 16 * h)
                    v_at.inc(vector.tensor_copy(out=atst[:], in_=ps_tr[:]))

            for h in range(NH):
                w = WH[h]
                for cb in range(2):
                    if cb == 1:
                        attn_evac_front(vector, h, 0)
                        attn_evac_back(vector, h, 0)
                    for u in range(UA[h] if cb == 0 else UB[h]):
                        for t in range(4):
                            for s in range(512 // w):
                                slot = sc_slot(n_sc, w)
                                n_sc += 1
                                if u == 0 and (_mask_delta(t, s, w) > -128):
                                    vector.wait_ge(t_sc.h, n_sc)
                                    d = _mask_delta(t, s, w)
                                    v_mk.inc(vector.tensor_tensor(
                                        out=slot, in0=slot,
                                        in1=m0m[:, 512 - d:512 - d + w], op=ALU.add))
                            n_av += 1
                attn_evac_front(vector, h, 1)
                attn_evac_back(vector, h, 1)
            for cb in range(2):
                vector.wait_ge(s_x5.h, 16 * (cb + 1))
                for es in range(16):
                    for st in range(4):
                        gi = cb * 64 + es * 4 + st
                        vector.wait_ge(t_op.h, gi + 1)
                        v_op.inc(vector.tensor_tensor(
                            out=xbuf[:, st * D + es * 256:st * D + (es + 1) * 256],
                            in0=slot4(gi),
                            in1=xbuf[:, st * D + es * 256:st * D + (es + 1) * 256],
                            op=ALU.add))
                v_op.inc(vector.tensor_copy(out=epst[:], in_=epst[:]))
                if cb == 0:
                    ffn_vec(vector, f2A, waits=[(v_op.h, 65)], prep=False)
                else:
                    ffn_vec(vector, f2B, waits=[(v_op.h, 130)], prep=False)

        @block.scalar
        def _(scalar):
            ln_scal(scalar, fA)
            ln_scal(scalar, fB)
            for k in range(32):
                scalar.wait_ge(s_qpc.h, 16 * (k + 1))
                s_qpk.inc(scalar.dma_start(
                    out=wqb_d.ap()[k * 128:(k + 1) * 128, D:D + 256],
                    in_=kvb[k % 2][:]), 16)
            for j in range(32):
                q_store(scalar, j)
            ln_scal(scalar, qB)
            ln_scal(scalar, qA)
            scalar.wait_ge(s_cc.h, 2)
            scalar.wait_ge(qS["Ap"].h, qA.a("Ap", 4))
            scalar.wait_ge(qS["Qm"].h, qA.a("Qm", 32))
            with scalar.register("r_ks") as r_ks:
                for i in range(12, 24):
                    scalar.reg_load(r_ks, offt[0:1, i:i + 1])
                    s_kv.inc(scalar.dma_start(
                        out=ktit[:, i * 512:(i + 1) * 512],
                        in_=bass.AP(g_kout, r_ks, [[1024, P], [1, 512]])), 16)
                scalar.wait_ge(s_cc.h, 3)
                for i in range(18, 24):
                    scalar.reg_load(r_ks, offt[0:1, 24 + i:25 + i])
                    s_kv.inc(scalar.dma_start(
                        out=viti[:, i * 516:(i + 1) * 516].rearrange(
                            "p (t v) -> p t v", v=129),
                        in_=bass.AP(g_vout, r_ks, [[129, P], [129 * P, 4], [1, 129]])), 16)
            scalar.wait_ge(s_tbl.h, 16)
            scalar.wait_ge(s_kv.h, 16 * 36)
            scalar.wait_ge(s_kvg.h, 16 * 12)
            n_sc = 0
            n_mk = 0
            n_av = 0
            buf_last_use = {t: 0 for t in range(4)}
            for h in range(NH):
                w = WH[h]
                ns = 512 // w
                scalar.wait_ge(s_bt[h % 2].h, 16 * (h // 2 + 1))
                uu = 0
                for cb in range(2):
                    for u in range(UA[h] if cb == 0 else UB[h]):
                        for t in range(4):
                            for s in range(ns):
                                slot = sc_slot(n_sc, w)
                                n_sc += 1
                                dm = (u == 0) and (_mask_delta(t, s, w) > -128)
                                if dm:
                                    n_mk += 1
                                    scalar.wait_ge(v_mk.h, n_mk)
                                else:
                                    scalar.wait_ge(t_sc.h, n_sc)
                                if s == 0 and buf_last_use[t] > 0:
                                    scalar.wait_ge(t_av.h, buf_last_use[t])
                                col = (uu * 4 + t) * ns + s
                                a_ex.inc(scalar.activation(
                                    out=probT[t][:, s * w:(s + 1) * w],
                                    in_=slot,
                                    func=AF.Exp, bias=btb[h % 2][:, col:col + 1],
                                    scale=SCALE))
                            n_av += 1
                            buf_last_use[t] = n_av
                        uu += 1
            scalar.wait_ge(t_av.h, PLAN.avB[NH - 1])
            s_x5.inc(scalar.dma_start(
                out=xbuf[:].rearrange("p (st d) -> p st d", d=D),
                in_=xpg.ap()[0:512, :].rearrange("(st p) d -> p st d", p=P)), 16)
            ln_scal(scalar, f2A)
            scalar.wait_ge(fS["Bm"].h, f2A.a("Bm", 64))
            s_a5.inc(scalar.dma_start(out=xlt[:], in_=at_d.ap()), 16)
            scalar.wait_ge(s_out.h, 16)
            s_x5.inc(scalar.dma_start(
                out=xbuf[:].rearrange("p (st d) -> p st d", d=D),
                in_=xpg.ap()[512:1024, :].rearrange("(st p) d -> p st d", p=P)), 16)
            ln_scal(scalar, f2B)

        @block.tensor
        def _(tensor):
            tensor.wait_ge(s_cast.h, 32)
            tensor.wait_ge(s_ld.h, 16)
            for dt in range(16):
                ins = tensor.matmul(out=ps_tr[:, dt * 32:(dt + 1) * 32],
                                    lhsT=enc0_bf[:, dt * 128:(dt + 1) * 128],
                                    rhs=ident[0:32, 0:32], start=True, stop=True)
            s_encT.inc(ins)
            for dt in range(16, 32):
                ins = tensor.matmul(out=ps_x[:, (dt - 16) * 32:(dt - 15) * 32],
                                    lhsT=enc0_bf[:, dt * 128:(dt + 1) * 128],
                                    rhs=ident[0:32, 0:32], start=True, stop=True)
            s_encT.inc(ins)
            tensor.wait_ge(s_encE.h, 2)
            for dt in range(32):
                ins = tensor.matmul(out=ps_tr[:, dt * 8:(dt + 1) * 8],
                                    lhsT=enc1_bf[:, dt * 128:(dt + 1) * 128],
                                    rhs=ident[0:8, 0:8], start=True, stop=True)
            s_encT.inc(ins)
            tensor.wait_ge(s_encE.h, 3)
            tensor.wait_ge(s_wsum.h, 1)
            tensor.wait_ge(s_cast.h, 64)
            tensor.wait_ge(s_ld.h, 48)
            for ot in range(32):
                ins = tensor.matmul(out=ps_x[0:32, 0:512],
                                    lhsT=w0s[:, ot * 32:(ot + 1) * 32],
                                    rhs=m_bf[:, ot * 512:(ot + 1) * 512],
                                    start=(ot == 0), stop=(ot == 31))
            s_gmm.inc(ins)
            for ot in range(32):
                ins = tensor.matmul(out=ps_tr[0:8, 0:512],
                                    lhsT=w1s[:, ot * 8:(ot + 1) * 8],
                                    rhs=m_bf[:, ot * 512:(ot + 1) * 512],
                                    start=(ot == 0), stop=(ot == 31))
            s_gmm.inc(ins)
            tensor.wait_ge(s_gev.h, 2)
            ffn_tens(tensor, fA, prep=True, gl_wait=True)
            ffn_tens(tensor, fB, prep=False, gl_wait=False,
                     pre=((s_wpe.h, 256),))
            qkv_tens(tensor, qB, 1, pre=((fS["Ev"].h, fB.a("Ev", 2)),))
            qkv_tens(tensor, qA, 0, pre=((qS["Ke"].h, qB.a("Ke", 5)),
                                         (qS["Qe"].h, qB.a("Qe", 32))))
            qA.w(tensor, "Qe", 32)
            qA.w(tensor, "Ke", 5)
            fB.w(tensor, "Be", 64)
            tensor.wait_ge(s_kv.h, 16 * 36)
            tensor.wait_ge(s_kvg.h, 16 * 12)
            n_sc = 0
            n_ex = 0
            n_av = 0
            for h in range(NH):
                w = WH[h]
                ns = 512 // w
                tensor.wait_ge(s_q[h % 2].h, 16 * (h // 2 + 1))
                if h > 0:
                    tensor.wait_ge(v_at.h, 2 * h)
                LAG = 2
                pend = []

                def emit_av(tensor, i, t, n_ex_t, stop):
                    tensor.wait_ge(a_ex.h, n_ex_t)
                    for j in range(4):
                        st_flag = _is_diag_slot(i) and (t == 0)
                        ins = tensor.matmul(
                            out=ps4[:, j * 512:j * 512 + 129],
                            lhsT=probT[t][:, j * 128:(j + 1) * 128],
                            rhs=viti[:, i * 516 + t * 129:i * 516 + (t + 1) * 129],
                            start=st_flag, stop=stop,
                            skip_group_check=True)
                    t_av.inc(ins)

                for cb in range(2):
                    nu = UA[h] if cb == 0 else UB[h]
                    if cb == 1:
                        for (pi, pt, pn, pl) in pend:
                            emit_av(tensor, pi, pt, pn, pl)
                        pend = []
                        tensor.wait_ge(v_an.h, 2 * h + 1)
                        if h > 0:
                            tensor.wait_ge(v_at.h, 2 * h)
                        for j in range(4):
                            ins = tensor.matmul(
                                out=ps_tr[:, j * 128:(j + 1) * 128],
                                lhsT=anat[:, j * 128:(j + 1) * 128],
                                rhs=ident[:], start=True, stop=True)
                        t_tr.inc(ins)
                    for u in range(nu):
                        i = u if cb == 0 else 8 + u
                        for t in range(4):
                            for s in range(ns):
                                if n_sc >= 3:
                                    tensor.wait_ge(a_ex.h, n_sc - 2)
                                t_sc.inc(tensor.matmul(
                                    out=sc_slot(n_sc, w),
                                    lhsT=ktit[:, i * 512 + t * 128:i * 512 + (t + 1) * 128],
                                    rhs=qbuf[h % 2][:, cb * 512 + s * w:cb * 512 + (s + 1) * w],
                                    start=True, stop=True))
                                n_sc += 1
                            n_ex += ns
                            pend.append((i, t, n_ex,
                                         cb == 1 and u == nu - 1 and t == 3))
                            n_av += 1
                            if len(pend) > LAG:
                                pi, pt, pn, pl = pend.pop(0)
                                emit_av(tensor, pi, pt, pn, pl)
                for (pi, pt, pn, pl) in pend:
                    emit_av(tensor, pi, pt, pn, pl)
                pend = []
                idx = h * 2 + 1
                tensor.wait_ge(v_an.h, idx + 1)
                tensor.wait_ge(v_at.h, idx)
                for j in range(4):
                    ins = tensor.matmul(out=ps_tr[:, j * 128:(j + 1) * 128],
                                        lhsT=anat[:, j * 128:(j + 1) * 128],
                                        rhs=ident[:], start=True, stop=True)
                t_tr.inc(ins)
            for cb in range(2):
                if cb == 0:
                    tensor.wait_ge(v_at.h, 64)
                else:
                    f2A.w(tensor, "Be", 64)
                    tensor.wait_ge(s_a5.h, 16)
                for es in range(16):
                    gi = cb * 16 + es
                    tensor.wait_ge(s_w5[gi % 2].h, 16 * (gi // 2 + 1))
                    wts = wt0 if gi % 2 == 0 else wt1
                    for st in range(4):
                        ti = gi * 4 + st
                        if cb == 0 and ti >= 4:
                            tensor.wait_ge(v_op.h, ti - 3)
                        elif cb == 1 and ti >= 68:
                            tensor.wait_ge(v_op.h, ti - 2)
                        pa = slot4(ti)
                        for ht in range(32):
                            ins = tensor.matmul(
                                out=pa,
                                lhsT=xlt[:, ht * 512 + st * 128:ht * 512 + (st + 1) * 128],
                                rhs=wts[:, ht * 256:(ht + 1) * 256],
                                start=(ht == 0), stop=(ht == 31))
                        t_op.inc(ins)
                if cb == 0:
                    ffn_tens(tensor, f2A, prep=False, gl_wait=False,
                             pre=((v_op.h, 65),))
                else:
                    ffn_tens(tensor, f2B, prep=False, gl_wait=False,
                             pre=((v_op.h, 130),))

    return nc


_CACHE = {}


def kernel(**inputs):
    x = np.asarray(inputs["L0_attn_hidden_states"])[0]
    wqkv = np.ascontiguousarray(inputs["L0_attn_w_qkv"], np.float32)
    wcv = np.ascontiguousarray(inputs["L0_attn_w_c"], np.float32)
    wbase = np.ascontiguousarray(inputs["L0_ffn_W_base"], np.float32)
    bbase = np.ascontiguousarray(inputs["L0_ffn_b_base"], np.float32).reshape(1, D)
    enc0 = np.ascontiguousarray(inputs["L0_ffn_enc0"], np.float32)
    enc1 = np.ascontiguousarray(inputs["L0_ffn_enc1"], np.float32)
    coefs0 = np.ascontiguousarray(inputs["L0_ffn_coefs0"], np.float32)
    coefs1 = np.ascontiguousarray(inputs["L0_ffn_coefs1"], np.float32)
    mask0 = np.ascontiguousarray(inputs["L0_ffn_mask0"]).view(np.uint8)
    mask1 = np.ascontiguousarray(inputs["L0_ffn_mask1"]).view(np.uint8)

    if "nc" not in _CACHE:
        _CACHE["nc"] = build_nc()
        _CACHE["M"] = _rfom_matrix()
        _CACHE["m0"] = _m0_mask()
        _CACHE["ident"] = np.eye(P).astype(BFNP)
        _CACHE["ones"] = np.ones((1, P), BFNP)
        _CACHE["tables"] = [_host_tables(c) for c in range(NCORE)]
    nc = _CACHE["nc"]
    M = _CACHE["M"]

    in_maps = []
    for c in range(NCORE):
        off, bias = _CACHE["tables"][c]
        xs = np.concatenate([x[512 * c:512 * (c + 1)],
                             x[512 * (15 - c):512 * (16 - c)]], axis=0)
        in_maps.append(dict(
            xin=np.ascontiguousarray(xs, np.float32),
            wqkv=wqkv, wc=wcv, wbase=wbase, bbase=bbase,
            enc0=enc0, enc1=enc1, coefs0=coefs0, coefs1=coefs1,
            cmask0=mask0, cmask1=mask1,
            msl=np.ascontiguousarray(M[:, 512 * c:512 * (c + 1)]),
            biastbl=bias, m0in=_CACHE["m0"],
            identin=_CACHE["ident"], onesin=_CACHE["ones"], offin=off,
        ))
    res = run_bass_kernel_spmd(nc, in_maps, core_ids=list(range(NCORE)))
    out = np.empty((S, D), np.float32)
    for c in range(NCORE):
        xo = np.asarray(res.results[c]["xout"])
        out[512 * c:512 * (c + 1)] = xo[0:512]
        out[512 * (15 - c):512 * (16 - c)] = xo[512:1024]
    return out.reshape(1, S, D)


if __name__ == "__main__":
    build_nc()
    print("build ok")
